# version 08
# speedup vs baseline: 1.0203x; 1.0026x over previous
import contextlib
import numpy as np
import ml_dtypes
import concourse.bass as bass
import concourse.mybir as mybir
from concourse.bass_utils import run_bass_kernel_spmd

F32 = mybir.dt.float32
BF16 = mybir.dt.bfloat16
I32 = mybir.dt.int32
AF = mybir.ActivationFunctionType
ALU = mybir.AluOpType
AX = mybir.AxisListType
bf16 = ml_dtypes.bfloat16

ENGINES = ("sync", "tensor", "vector", "scalar", "gpsimd")

B, T, D = 2, 4096, 2048
GT = B * T
TOK = GT // 8
NCHUNK = D // 128


class Prog:

    def __init__(self, nc, stack):
        self.nc = nc
        self.stack = stack
        self.streams = {e: [] for e in ENGINES}
        self.semh = {}
        self.semv = {}
        self.alias = {}

    def sem(self, name):
        name = self.alias.get(name, name)
        if name not in self.semh:
            self.semh[name] = self.stack.enter_context(self.nc.semaphore(name))
            self.semv[name] = 0
        return self.semh[name]

    def val(self, name):
        self.sem(name)
        return self.semv[self.alias.get(name, name)]

    def _wait(self, eng, name, v):
        if v <= 0:
            return
        h = self.sem(name)
        name = self.alias.get(name, name)
        self.streams[eng].append(lambda e, h=h, v=v: e.wait_ge(h, v))

    def wait(self, eng, tok):
        if tok is None:
            return
        name, v = tok
        if self.alias.get(name, name) == "ser_" + eng:
            return
        self._wait(eng, name, v)

    def wait_all(self, eng, name):
        self._wait(eng, name, self.val(name))

    def op(self, eng, fn, chain=True):
        ser = "ser_" + eng
        h = self.sem(ser)
        ser = self.alias.get(ser, ser)
        if chain and eng != "tensor" and self.semv[ser] > 0:
            self._wait(eng, ser, self.semv[ser])
        self.semv[ser] += 1

        def g(e, fn=fn, h=h):
            fn(e).then_inc(h, 1)
        self.streams[eng].append(g)
        return (ser, self.semv[ser])

    def dma(self, eng, out, in_, sem, **kw):
        h = self.sem(sem)
        sem = self.alias.get(sem, sem)
        self.semv[sem] += 16

        def g(e, o=out, i=in_, h=h, kw=kw):
            e.dma_start(out=o, in_=i, **kw).then_inc(h, 16)
        self.streams[eng].append(g)
        return (sem, self.semv[sem])

    def barrier(self):
        vals = dict(self.semv)
        for eng in ENGINES:
            for n, v in vals.items():
                if n == "ser_" + eng:
                    continue
                self._wait(eng, n, v)

    def realize(self):
        with self.nc.Block() as blk:
            @blk.sync
            def _(e):
                for f in self.streams["sync"]:
                    f(e)

            @blk.tensor
            def _(e):
                for f in self.streams["tensor"]:
                    f(e)

            @blk.vector
            def _(e):
                for f in self.streams["vector"]:
                    f(e)

            @blk.scalar
            def _(e):
                for f in self.streams["scalar"]:
                    f(e)

            @blk.gpsimd
            def _(e):
                for f in self.streams["gpsimd"]:
                    f(e)


def build_phase_A(upto=99, dtskip=()):
    nc = bass.Bass("TRN2", target_bir_lowering=False)
    XR = nc.dram_tensor("xr", [TOK, D], F32, kind="ExternalInput").ap()
    WBCT = nc.dram_tensor("wbcT", [D, 128], F32, kind="ExternalInput").ap()
    WDTT = nc.dram_tensor("wdtT", [64, D], F32, kind="ExternalInput").ap()
    BDT = nc.dram_tensor("bdt", [D, 1], F32, kind="ExternalInput").ap()
    W0T = nc.dram_tensor("w0T", [D, D], F32, kind="ExternalInput").ap()
    B0 = nc.dram_tensor("b0", [D, 1], F32, kind="ExternalInput").ap()
    IDB = nc.dram_tensor("identb", [128, 128], BF16, kind="ExternalInput").ap()

    XN = nc.dram_tensor("xn", [TOK, D], F32, kind="ExternalOutput").ap()
    DELTA = nc.dram_tensor("delta", [D, TOK], F32, kind="ExternalOutput").ap()
    BCO = nc.dram_tensor("bc", [64, TOK], F32, kind="ExternalOutput").ap()
    FFN0 = nc.dram_tensor("ffn0", [D, TOK], F32, kind="ExternalOutput").ap()
    TOTS = nc.dram_tensor("tots", [1, 1], F32, kind="Internal").ap()

    NT = TOK // 128

    with contextlib.ExitStack() as stack:
        sb = lambda name, shape, dt: stack.enter_context(nc.sbuf_tensor(name, shape, dt))
        ps = lambda name, shape, dt: stack.enter_context(nc.psum_tensor(name, shape, dt))

        xbuf = [sb(f"xbuf{i}", [128, D], F32) for i in range(2)]
        xnf = sb("xnf", [128, D], F32)
        xnb = sb("xnb", [128, D], BF16)
        xnT = sb("xnT", [128, NCHUNK * TOK], BF16)
        ident = sb("ident", [128, 128], BF16)
        mu = sb("mu", [128, 1], F32)
        mur = sb("mur", [128, 1], F32)
        sqa = sb("sqa", [128, 1], F32)
        var = sb("var", [128, 1], F32)
        sd = sb("sd", [128, 1], F32)
        rstd = sb("rstd", [128, 1], F32)
        t1 = sb("t1", [128, 1], F32)
        epsc = sb("epsc", [128, 1], F32)
        wbc_sb = sb("wbc_sb", [128, NCHUNK * 128], BF16)
        wdt_sb = sb("wdt_sb", [64, D], BF16)
        bdt_sb = sb("bdt_sb", [128, NCHUNK], F32)
        b0_sb = sb("b0_sb", [128, NCHUNK], F32)
        dbc = sb("dbc", [128, TOK], F32)
        dtrb = sb("dtrb", [64, TOK], BF16)
        spm = sb("spm", [128, TOK], F32)
        aa = sb("aa", [128, TOK], F32)
        dlt = sb("dlt", [128, TOK], F32)
        wld = [sb(f"wld{i}", [128, D], F32) for i in range(2)]
        absacc = sb("absacc", [128, 1], F32)
        abst = sb("abst", [128, 1], F32)
        ones = sb("ones", [128, 1], F32)
        tot = sb("tot", [1, 1], F32)
        totb = sb("totb", [128, 1], F32)
        thr = sb("thr", [128, 1], F32)
        s0c = sb("s0c", [128, 1], F32)
        awt = sb("awt", [128, D], F32)
        sgt = sb("sgt", [128, D], F32)
        get = sb("get", [128, D], F32)
        w0q = sb("w0q", [128, NCHUNK * D], BF16)
        ffb = sb("ffb", [128, TOK], F32)

        pst = [ps(f"pst{i}", [128, 128], BF16) for i in range(2)]
        psA = ps("psA", [128, 512], F32)
        psB = ps("psB", [128, 512], F32)
        ptot = ps("ptot", [1, 1], F32)

        p = Prog(nc, stack)

        p.dma("sync", ident[:], IDB, sem="ldc")
        for c in range(NCHUNK):
            p.dma("gpsimd", wbc_sb[:, c * 128:(c + 1) * 128],
                  WBCT[c * 128:(c + 1) * 128, :], sem="ldcg")
        p.dma("gpsimd", wdt_sb[:], WDTT, sem="ldcg")
        for c in range(NCHUNK):
            p.dma("sync", bdt_sb[:, c:c + 1], BDT[c * 128:(c + 1) * 128, :], sem="ldc")
            p.dma("sync", b0_sb[:, c:c + 1], B0[c * 128:(c + 1) * 128, :], sem="ldc")
        p.op("vector", lambda e: e.memset(epsc[:], 1e-5))

        t_xn = {}
        t_stxn = {}
        t_cp = {}
        ev_tok = {}
        for i in range(NT):
            xb = xbuf[i % 2]
            if i >= 2:
                p.wait("sync", t_xn[i - 2])
            lx = f"ld_x{i % 2}"
            ldtok = p.dma("sync", xb[:], XR[i * 128:(i + 1) * 128, :], sem=lx)

            p.wait("vector", ldtok)
            p.op("vector", lambda e, xb=xb: e.reduce_sum(mur[:], xb[:], axis=AX.X))
            p.op("vector", lambda e: e.tensor_scalar_mul(mu[:], mur[:], 1.0 / D))
            p.wait("scalar", ldtok)
            if i >= 1:
                p.wait("scalar", t_stxn[i - 1])
                p.wait("scalar", t_cp[i - 1])
            tsq = p.op("scalar", lambda e, xb=xb: e.activation(
                xnf[:], xb[:], AF.Square, accum_out=sqa[:]))
            p.op("vector", lambda e: e.tensor_mul(t1[:], mu[:], mu[:]))
            p.wait("vector", tsq)
            p.op("vector", lambda e: e.tensor_scalar_mul(var[:], sqa[:], 1.0 / D))
            tvar = p.op("vector", lambda e: e.tensor_sub(var[:], var[:], t1[:]))
            p.wait("scalar", tvar)
            tsd = p.op("scalar", lambda e: e.activation(sd[:], var[:], AF.Sqrt,
                                                        bias=epsc[:]))
            p.wait("vector", tsd)
            p.op("vector", lambda e: e.reciprocal(rstd[:], sd[:]))
            if i >= 1:
                p.wait("vector", t_stxn[i - 1])
            t_xn[i] = p.op("vector", lambda e, xb=xb: e.scalar_tensor_tensor(
                xnf[:], xb[:], mu[:], rstd[:].broadcast_to((128, D)),
                ALU.subtract, ALU.mult))
            p.wait("gpsimd", t_xn[i])
            t_stxn[i] = p.dma("gpsimd", XN[i * 128:(i + 1) * 128, :], xnf[:], sem="st_xn")
            tcp = t_cp[i] = p.op("vector", lambda e: e.tensor_copy(xnb[:], xnf[:]))
            p.wait_all("tensor", "ldc")
            p.wait_all("tensor", "ldcg")
            p.wait("tensor", tcp)
            for c in range(NCHUNK):
                g = 16 * i + c
                pt = pst[c % 2]
                if g >= 2:
                    p.wait("tensor", ev_tok[g - 2])
                ttp = p.op("tensor", lambda e, pt=pt, c=c: e.transpose(
                    pt[:], xnb[:, c * 128:(c + 1) * 128], ident[:]))
                p.wait("vector", ttp)
                ev_tok[g] = p.op("vector", lambda e, pt=pt, c=c, i=i: e.tensor_copy(
                    xnT[:, c * TOK + i * 128: c * TOK + (i + 1) * 128], pt[:]))

        if upto < 2:
            p.wait_all("gpsimd", "st_xn")
            p.realize()
            return nc
        p.wait("tensor", ev_tok[16 * NT - 1])
        p.wait("tensor", ev_tok[16 * NT - 2])
        mmtok = {}
        for j in range(2):
            pp = psA if j == 0 else psB
            for c in range(NCHUNK):
                mmtok[j] = p.op("tensor", lambda e, pp=pp, c=c, j=j: e.matmul(
                    pp[:], wbc_sb[:, c * 128:(c + 1) * 128],
                    xnT[:, c * TOK + j * 512: c * TOK + j * 512 + 512],
                    start=(c == 0), stop=(c == NCHUNK - 1)))
        ev_dbc = {}
        for j in range(2):
            pp = psA if j == 0 else psB
            p.wait("vector", mmtok[j])
            ev_dbc[j] = p.op("vector", lambda e, pp=pp, j=j: e.tensor_copy(
                dbc[:, j * 512:(j + 1) * 512], pp[:]))
        tdtr = p.op("vector", lambda e: e.tensor_copy(dtrb[:], dbc[0:64, :]))
        p.wait("gpsimd", ev_dbc[0])
        p.wait("gpsimd", ev_dbc[1])
        t_stbc = p.dma("gpsimd", BCO, dbc[64:128, :], sem="st_bc")

        if upto < 2.5:
            p.wait_all("gpsimd", "st_xn")
            p.wait_all("gpsimd", "st_bc")
            p.realize()
            return nc
        p.wait("tensor", tdtr)
        sp_v = {}
        sp_a = {}
        sp_dl = {}
        st_dl = {}
        nchunk_dt = 1 if upto < 2.7 else NCHUNK
        e_ts = "e_ts" not in dtskip
        e_sc = "e_sc" not in dtskip
        e_add = ("e_add" not in dtskip) and e_ts and e_sc
        e_st = ("e_st" not in dtskip) and e_add

        for n in range(nchunk_dt):
            for j in range(2):
                m = 2 * n + j
                pp = psA if m % 2 == 0 else psB
                if m >= 2:
                    if e_ts:
                        p.wait("tensor", sp_v[m - 2])
                    if e_sc:
                        p.wait("tensor", sp_a[m - 2])
                tmm = p.op("tensor", lambda e, pp=pp, n=n, j=j: e.matmul(
                    pp[:], wdt_sb[:, n * 128:(n + 1) * 128],
                    dtrb[:, j * 512:(j + 1) * 512], start=True, stop=True))
                if e_ts:
                    p.wait("vector", tmm)
                    if n >= 1 and j == 0 and e_st:
                        p.wait("vector", st_dl[n - 1])
                    sp_v[m] = p.op("vector", lambda e, pp=pp, n=n, j=j: e.tensor_scalar(
                        spm[:, j * 512:(j + 1) * 512], pp[:], bdt_sb[:, n:n + 1], 0.0,
                        ALU.add, ALU.max))
                if e_sc:
                    p.wait("scalar", tmm)
                    if e_ts:
                        p.wait("scalar", sp_v[m])
                    if m >= 2 and e_add:
                        p.wait("scalar", sp_dl[m - 2])
                    sp_a[m] = p.op("scalar", lambda e, pp=pp, n=n, j=j: e.activation(
                        aa[:, j * 512:(j + 1) * 512], pp[:], AF.Abs,
                        bias=bdt_sb[:, n:n + 1]))
                    p.op("scalar", lambda e, j=j: e.activation(
                        aa[:, j * 512:(j + 1) * 512], aa[:, j * 512:(j + 1) * 512],
                        AF.Exp, scale=-1.0))
                    tln = p.op("scalar", lambda e, j=j: e.activation(
                        aa[:, j * 512:(j + 1) * 512], aa[:, j * 512:(j + 1) * 512],
                        AF.Ln, bias=1.0))
                if e_add:
                    p.wait("vector", tln)
                    sp_dl[m] = p.op("vector", lambda e, j=j: e.tensor_add(
                        dlt[:, j * 512:(j + 1) * 512], spm[:, j * 512:(j + 1) * 512],
                        aa[:, j * 512:(j + 1) * 512]))
            if e_st:
                p.wait("gpsimd", sp_dl[2 * n])
                p.wait("gpsimd", sp_dl[2 * n + 1])
                st_dl[n] = p.dma("gpsimd", DELTA[n * 128:(n + 1) * 128, :], dlt[:],
                                 sem="st_dl")

        if upto < 3:
            p.wait_all("gpsimd", "st_xn")
            p.wait_all("gpsimd", "st_bc")
            p.wait_all("gpsimd", "st_dl")
            p.realize()
            return nc
        p.op("vector", lambda e: e.memset(ones[:], 1.0))
        w_used = {}
        for c in range(NCHUNK):
            wb = wld[c % 2]
            if c >= 2:
                p.wait("sync", w_used[c - 2])
            lw = f"ld_w{c % 2}"
            lwt = p.dma("sync", wb[:], W0T[c * 128:(c + 1) * 128, :], sem=lw)
            p.wait("vector", lwt)
            if c == 0:
                w_used[c] = p.op("vector", lambda e, wb=wb: e.reduce_sum(
                    absacc[:], wb[:], axis=AX.X, apply_absolute_value=True))
            else:
                p.op("vector", lambda e, wb=wb: e.reduce_sum(
                    abst[:], wb[:], axis=AX.X, apply_absolute_value=True))
                w_used[c] = p.op("vector", lambda e: e.tensor_add(
                    absacc[:], absacc[:], abst[:]))
        p.wait("tensor", w_used[NCHUNK - 1])
        tmt = p.op("tensor", lambda e: e.matmul(ptot[:], ones[:], absacc[:],
                                                start=True, stop=True))
        p.wait("vector", tmt)
        tev = p.op("vector", lambda e: e.tensor_copy(tot[:], ptot[:]))
        p.wait("gpsimd", tev)
        tst = p.dma("gpsimd", TOTS, tot[:], sem="st_tot")
        p.wait("gpsimd", tst)
        tld = p.dma("gpsimd", totb[:], TOTS.broadcast_to((128, 1)), sem="ld_tot")
        p.wait("vector", tld)
        p.op("vector", lambda e: e.tensor_scalar_mul(thr[:], totb[:], 0.5 / (D * D)))
        ts0 = p.op("vector", lambda e: e.tensor_scalar_mul(s0c[:], totb[:], 1.0 / (D * D)))

        q_sg = {}
        q_ge = {}
        q_wq = {}
        for c in range(NCHUNK):
            wb = wld[c % 2]
            if c >= 2:
                p.wait("sync", q_sg[c - 2])
            else:
                p.wait("sync", w_used[NCHUNK - 2 + c])
            lw = f"ld_w{c % 2}"
            lwt = p.dma("sync", wb[:], W0T[c * 128:(c + 1) * 128, :], sem=lw)
            p.wait("scalar", lwt)
            if c >= 1:
                p.wait("scalar", q_ge[c - 1])
                p.wait("scalar", q_wq[c - 1])
            p.op("scalar", lambda e, wb=wb: e.activation(awt[:], wb[:], AF.Abs))
            q_sg[c] = p.op("scalar", lambda e, wb=wb: e.activation(sgt[:], wb[:], AF.Sign))
            p.wait("vector", q_sg[c])
            q_ge[c] = p.op("vector", lambda e: e.tensor_single_scalar(
                get[:], awt[:], thr[:], ALU.is_ge))
            q_wq[c] = p.op("vector", lambda e, c=c: e.tensor_mul(
                w0q[:, c * D:(c + 1) * D], get[:], sgt[:]))

        p.wait("tensor", q_wq[NCHUNK - 1])
        ff_ev = {}
        st_ff = {}
        for n in range(NCHUNK):
            for j in range(2):
                m = 2 * n + j
                pp = psA if m % 2 == 0 else psB
                if m >= 2:
                    p.wait("tensor", ff_ev[m - 2])
                for c in range(NCHUNK):
                    tmm = p.op("tensor", lambda e, pp=pp, n=n, j=j, c=c: e.matmul(
                        pp[:], w0q[:, c * D + n * 128: c * D + (n + 1) * 128],
                        xnT[:, c * TOK + j * 512: c * TOK + j * 512 + 512],
                        start=(c == 0), stop=(c == NCHUNK - 1)))
                p.wait("vector", tmm)
                if n >= 1 and j == 0:
                    p.wait("vector", st_ff[n - 1])
                ff_ev[m] = p.op("vector", lambda e, pp=pp, n=n, j=j: e.scalar_tensor_tensor(
                    ffb[:, j * 512:(j + 1) * 512], pp[:], s0c[:],
                    b0_sb[:, n:n + 1].broadcast_to((128, 512)),
                    ALU.mult, ALU.add))
            p.wait("gpsimd", ff_ev[2 * n])
            p.wait("gpsimd", ff_ev[2 * n + 1])
            st_ff[n] = p.dma("gpsimd", FFN0[n * 128:(n + 1) * 128, :], ffb[:], sem="st_ff")

        p.wait_all("gpsimd", "st_xn")
        p.wait_all("gpsimd", "st_bc")
        p.wait_all("gpsimd", "st_dl")
        p.wait_all("gpsimd", "st_ff")

        p.realize()
    return nc


def _phase_A_inmaps(inputs):
    x2d = np.ascontiguousarray(inputs["L0_attn_x"].reshape(GT, D))
    wbcT = np.ascontiguousarray(inputs["L0_attn_W_bc"].T)
    wdtT = np.ascontiguousarray(inputs["L0_attn_W_dt"].T)
    bdt = np.ascontiguousarray(inputs["L0_attn_b_dt"].reshape(D, 1))
    w0T = np.ascontiguousarray(inputs["L0_ffn_weight"].T)
    b0 = np.ascontiguousarray(inputs["L0_ffn_bias"].reshape(D, 1))
    idb = np.eye(128, dtype=bf16)
    return [{"xr": x2d[c * TOK:(c + 1) * TOK], "wbcT": wbcT, "wdtT": wdtT,
             "bdt": bdt, "w0T": w0T, "b0": b0, "identb": idb} for c in range(8)]


def run_phase_A(inputs):
    nc = build_phase_A()
    res = run_bass_kernel_spmd(nc, _phase_A_inmaps(inputs), core_ids=list(range(8)))
    xn = np.concatenate([r["xn"] for r in res.results], axis=0)
    delta = np.concatenate([r["delta"] for r in res.results], axis=1)
    bc = np.concatenate([r["bc"] for r in res.results], axis=1)
    ffn0 = np.concatenate([r["ffn0"] for r in res.results], axis=1)
    return xn, delta, bc, ffn0




TC_B = 128
NCH_B = T // TC_B
NB_B = 4


def build_phase_B():
    nc = bass.Bass("TRN2", target_bir_lowering=False)
    DL = nc.dram_tensor("dl", [512, T], F32, kind="ExternalInput").ap()
    U = nc.dram_tensor("u", [512, T], F32, kind="ExternalInput").ap()
    BT = nc.dram_tensor("bt", [T, 32], F32, kind="ExternalInput").ap()
    CT = nc.dram_tensor("ct", [T, 32], F32, kind="ExternalInput").ap()
    ALOG = nc.dram_tensor("alog", [512, 32], F32, kind="ExternalInput").ap()
    DPAR = nc.dram_tensor("dpar", [512, 1], F32, kind="ExternalInput").ap()
    Y = nc.dram_tensor("y", [512, T], F32, kind="ExternalOutput").ap()

    F = TC_B * 32

    with contextlib.ExitStack() as stack:
        sb = lambda name, shape, dt: stack.enter_context(nc.sbuf_tensor(name, shape, dt))

        dlb = [sb(f"dlb{i}", [128, TC_B], F32) for i in range(2)]
        ub = [sb(f"ub{i}", [128, TC_B], F32) for i in range(2)]
        du = sb("du", [128, TC_B], F32)
        dA = sb("dA", [128, F], F32)
        dBu = sb("dBu", [128, F], F32)
        hh = sb("hh", [128, F], F32)
        bbc = [sb(f"bbc{i}", [128, F], F32) for i in range(2)]
        cbc = [sb(f"cbc{i}", [128, F], F32) for i in range(2)]
        yb = [sb(f"yb{i}", [128, TC_B], F32) for i in range(2)]
        yr = sb("yr", [128, TC_B], F32)
        negA = sb("negA", [128, NB_B * 32], F32)
        dcol = sb("dcol", [128, NB_B], F32)
        state = sb("state", [128, NB_B * 32], F32)

        p = Prog(nc, stack)

        for b in range(NB_B):
            tl = p.dma("sync", negA[:, b * 32:(b + 1) * 32],
                       ALOG[b * 128:(b + 1) * 128, :], sem=f"ld_al{b}")
            p.wait("scalar", tl)
            texp = p.op("scalar", lambda e, b=b: e.activation(
                negA[:, b * 32:(b + 1) * 32], negA[:, b * 32:(b + 1) * 32], AF.Exp))
            p.wait("vector", texp)
            tn = p.op("vector", lambda e, b=b: e.tensor_scalar_mul(
                negA[:, b * 32:(b + 1) * 32], negA[:, b * 32:(b + 1) * 32], -1.0))
            p.wait("scalar", tn)
            tdp = p.dma("sync", dcol[:, b:b + 1], DPAR[b * 128:(b + 1) * 128, :],
                        sem=f"ld_dp{b}")
            p.wait("vector", tdp)
        tst = p.op("vector", lambda e: e.memset(state[:], 0.0))

        def bc_src(M, c):
            sl = M[c * TC_B:(c + 1) * TC_B, :]
            return sl.rearrange("t n -> (t n)").unsqueeze(0).broadcast_to((128, F))

        tb = {0: p.dma("sync", bbc[0][:], bc_src(BT, 0), sem="ld_b0")}
        tc = {0: p.dma("sync", cbc[0][:], bc_src(CT, 0), sem="ld_c0")}

        dbu_tok = {}
        ym_tok = {}
        yfin = {}
        scan_last = {}
        st_y = {}
        exp_last = {}

        for c in range(NCH_B):
            if c + 1 < NCH_B:
                if c >= 1:
                    for b in range(NB_B):
                        p.wait("sync", dbu_tok[(c - 1, b)])
                tb[c + 1] = p.dma("sync", bbc[(c + 1) % 2][:], bc_src(BT, c + 1),
                                  sem=f"ld_b{(c + 1) % 2}")
                if c >= 1:
                    for b in range(NB_B):
                        p.wait("sync", ym_tok[(c - 1, b)])
                tc[c + 1] = p.dma("sync", cbc[(c + 1) % 2][:], bc_src(CT, c + 1),
                                  sem=f"ld_c{(c + 1) % 2}")
            for b in range(NB_B):
                it = c * NB_B + b
                if it >= 2:
                    p.wait("sync", yfin[it - 2])
                    p.wait("sync", exp_last[it - 2])
                tdl = p.dma("sync", dlb[it % 2][:],
                            DL[b * 128:(b + 1) * 128, c * TC_B:(c + 1) * TC_B],
                            sem=f"ld_dl{it % 2}")
                tu = p.dma("sync", ub[it % 2][:],
                           U[b * 128:(b + 1) * 128, c * TC_B:(c + 1) * TC_B],
                           sem=f"ld_u{it % 2}")
                p.wait("vector", tdl)
                p.wait("vector", tu)
                tdu = p.op("vector", lambda e, it=it: e.tensor_mul(
                    du[:], dlb[it % 2][:], ub[it % 2][:]))
                p.wait("scalar", tdl)
                if it >= 1:
                    p.wait("scalar", scan_last[it - 1])
                dA3 = dA.ap().rearrange("p (t n) -> p t n", n=32)
                for n in range(32):
                    exp_last[it] = p.op("scalar", lambda e, n=n, b=b, it=it: e.activation(
                        dA3[:, :, n], dlb[it % 2][:], AF.Exp,
                        scale=negA[:, b * 32 + n:b * 32 + n + 1]))
                p.wait("vector", tb[c])
                dbu_tok[(c, b)] = p.op("vector", lambda e, c=c: e.tensor_mul(
                    dBu[:], du.ap().unsqueeze(2).broadcast_to((128, TC_B, 32)),
                    bbc[c % 2][:].rearrange("p (t n) -> p t n", n=32)))
                p.wait("vector", exp_last[it])
                dBu3 = dBu.ap().rearrange("p (t n) -> p t n", n=32)
                hh3 = hh.ap().rearrange("p (t n) -> p t n", n=32)
                for n in range(32):
                    scan_last[it] = p.op("vector", lambda e, n=n, b=b: e.tensor_tensor_scan(
                        hh3[:, :, n], dA3[:, :, n], dBu3[:, :, n],
                        state[:, b * 32 + n:b * 32 + n + 1], ALU.mult, ALU.add))
                p.op("vector", lambda e, b=b: e.tensor_copy(
                    state[:, b * 32:(b + 1) * 32], hh[:, (TC_B - 1) * 32:TC_B * 32]))
                p.wait("vector", tc[c])
                ym_tok[(c, b)] = p.op("vector", lambda e, c=c: e.tensor_mul(
                    hh[:], hh[:], cbc[c % 2][:]))
                p.op("vector", lambda e: e.reduce_sum(
                    yr[:], hh.ap().rearrange("p (t n) -> p t n", n=32), axis=AX.X))
                if it >= 2:
                    p.wait("vector", st_y[it - 2])
                yfin[it] = p.op("vector", lambda e, it=it, b=b: e.scalar_tensor_tensor(
                    yb[it % 2][:], ub[it % 2][:], dcol[:, b:b + 1], yr[:],
                    ALU.mult, ALU.add))
                p.wait("gpsimd", yfin[it])
                st_y[it] = p.dma("gpsimd",
                                 Y[b * 128:(b + 1) * 128, c * TC_B:(c + 1) * TC_B],
                                 yb[it % 2][:], sem=f"st_y{it % 2}")

        p.wait_all("gpsimd", "st_y0")
        p.wait_all("gpsimd", "st_y1")
        p.realize()
    return nc


def _phase_B_inmaps(inputs, xn, delta, bc):
    maps = []
    for core in range(8):
        b = core // 4
        d0 = (core % 4) * 512
        uT = np.ascontiguousarray(xn[b * T:(b + 1) * T, d0:d0 + 512].T)
        dl = np.ascontiguousarray(delta[d0:d0 + 512, b * T:(b + 1) * T])
        bt = np.ascontiguousarray(bc[0:32, b * T:(b + 1) * T].T)
        ct = np.ascontiguousarray(bc[32:64, b * T:(b + 1) * T].T)
        alog = np.ascontiguousarray(inputs["L0_attn_A_log"][d0:d0 + 512, :])
        dpar = np.ascontiguousarray(inputs["L0_attn_D_param"][d0:d0 + 512].reshape(512, 1))
        maps.append({"dl": dl, "u": uT, "bt": bt, "ct": ct, "alog": alog,
                     "dpar": dpar})
    return maps


def run_phase_B(inputs, xn, delta, bc):
    nc = build_phase_B()
    res = run_bass_kernel_spmd(nc, _phase_B_inmaps(inputs, xn, delta, bc),
                               core_ids=list(range(8)))
    y = np.zeros((2, 2048, T), np.float32)
    for core in range(8):
        b = core // 4
        d0 = (core % 4) * 512
        y[b, d0:d0 + 512] = res.results[core]["y"]
    return y




def build_phase_C():
    nc = bass.Bass("TRN2", target_bir_lowering=False)
    XR = nc.dram_tensor("xr", [TOK, D], F32, kind="ExternalInput").ap()
    YR = nc.dram_tensor("yr", [TOK, D], F32, kind="ExternalInput").ap()
    F0 = nc.dram_tensor("f0", [TOK, D], F32, kind="ExternalInput").ap()
    HLX = nc.dram_tensor("hlx", [3, D], F32, kind="ExternalInput").ap()
    WT = {s: nc.dram_tensor(f"w{s}T", [D, D], F32, kind="ExternalInput").ap()
          for s in "rkvg"}
    TM = {s: nc.dram_tensor(f"tm{s}", [D, 1], F32, kind="ExternalInput").ap()
          for s in "rkvg"}
    IDB = nc.dram_tensor("identb", [128, 128], BF16, kind="ExternalInput").ap()

    X1 = nc.dram_tensor("x1", [TOK, D], F32, kind="ExternalOutput").ap()
    XN1T = nc.dram_tensor("xn1T", [D, TOK], BF16, kind="ExternalOutput").ap()
    OUT = {s: nc.dram_tensor(f"o{s}", [D, TOK], F32, kind="ExternalOutput").ap()
           for s in "rkvg"}
    HSC = nc.dram_tensor("hsc", [D, 1], BF16, kind="Internal").ap()

    NT = TOK // 128
    TOKH = TOK + 1

    with contextlib.ExitStack() as stack:
        sb = lambda name, shape, dt: stack.enter_context(nc.sbuf_tensor(name, shape, dt))
        ps = lambda name, shape, dt: stack.enter_context(nc.psum_tensor(name, shape, dt))

        xb = [sb(f"xb{i}", [128, D], F32) for i in range(2)]
        yb = [sb("ybc0", [128, D], F32)] * 2
        fb = [sb("fbc0", [128, D], F32)] * 2
        x1f = sb("x1f", [128, D], F32)
        x1b = sb("x1b", [128, D], BF16)
        xnT = sb("xnTc", [128, NCHUNK * TOKH], BF16)
        ident = sb("identc", [128, 128], BF16)
        mu = sb("muc", [128, 1], F32)
        mur = sb("murc", [128, 1], F32)
        sqa = sb("sqac", [128, 1], F32)
        var = sb("varc", [128, 1], F32)
        sd = sb("sdc", [128, 1], F32)
        rstd = sb("rstdc", [128, 1], F32)
        t1 = sb("t1c", [128, 1], F32)
        epsc = sb("epscc", [128, 1], F32)
        h1 = sb("h1", [1, D], F32)
        h1b = sb("h1b", [1, D], BF16)
        hmu = sb("hmu", [1, 1], F32)
        hvar = sb("hvar", [1, 1], F32)
        hsd = sb("hsd", [1, 1], F32)
        hrs = sb("hrs", [1, 1], F32)
        ht1 = sb("ht1", [1, 1], F32)
        tmt = sb("tmt", [128, 4 * NCHUNK], F32)
        omt = sb("omt", [128, 4 * NCHUNK], F32)
        xsb = sb("xsb", [128, NCHUNK * TOK], BF16)
        xtmp = sb("xtmp", [128, TOK], F32)
        wsb = sb("wsb", [128, NCHUNK * D], BF16)
        ob = sb("ob", [128, TOK], F32)

        pst = [ps(f"pstc{i}", [128, 128], BF16) for i in range(2)]
        psA = ps("psAc", [128, 512], F32)
        psB = ps("psBc", [128, 512], F32)

        p = Prog(nc, stack)

        p.dma("sync", ident[:], IDB, sem="ldc")
        for si, s in enumerate("rkvg"):
            for c in range(NCHUNK):
                p.dma("sync", tmt[:, si * NCHUNK + c:si * NCHUNK + c + 1],
                      TM[s][c * 128:(c + 1) * 128, :], sem="ldc")
        tmv = p.op("vector", lambda e: e.memset(epsc[:], 1e-5))
        p.wait_all("vector", "ldc")
        tom = p.op("vector", lambda e: e.tensor_scalar(
            omt[:], tmt[:], -1.0, 1.0, ALU.mult, ALU.add))

        th0 = p.dma("gpsimd", h1[:], HLX[0:1, :], sem="ld_h")
        p.wait("gpsimd", th0)
        th0 = p.dma("gpsimd", h1[:], HLX[1:2, :], sem="ld_h", accum_op=ALU.add)
        p.wait("gpsimd", th0)
        th1 = p.dma("gpsimd", h1[:], HLX[2:3, :], sem="ld_h", accum_op=ALU.add)
        p.wait("vector", th1)
        p.op("vector", lambda e: e.reduce_sum(hmu[:], h1[:], axis=AX.X))
        p.op("vector", lambda e: e.tensor_scalar_mul(hmu[:], hmu[:], 1.0 / D))
        p.wait("scalar", th1)
        tsq = p.op("scalar", lambda e: e.activation(x1f[0:1, :], h1[:], AF.Square,
                                                    accum_out=hvar[:]))
        p.wait("vector", tsq)
        p.op("vector", lambda e: e.tensor_scalar_mul(hvar[:], hvar[:], 1.0 / D))
        p.op("vector", lambda e: e.tensor_mul(ht1[:], hmu[:], hmu[:]))
        tv = p.op("vector", lambda e: e.tensor_sub(hvar[:], hvar[:], ht1[:]))
        p.wait("scalar", tv)
        tsd = p.op("scalar", lambda e: e.activation(hsd[:], hvar[:], AF.Sqrt,
                                                    bias=epsc[0:1, :]))
        p.wait("vector", tsd)
        p.op("vector", lambda e: e.reciprocal(hrs[:], hsd[:]))
        thn = p.op("vector", lambda e: e.scalar_tensor_tensor(
            h1b[:], h1[:], hmu[:], hrs[:].broadcast_to((1, D)),
            ALU.subtract, ALU.mult))
        p.wait("gpsimd", thn)
        tst = p.dma("gpsimd", HSC.rearrange("d one -> one d"), h1b[:], sem="st_h")
        p.wait("sync", tst)
        for c in range(NCHUNK):
            p.dma("sync", xnT[:, c * TOKH:c * TOKH + 1],
                  HSC[c * 128:(c + 1) * 128, :], sem="ld_hb")

        t_x1 = {}
        t_st = {}
        t_cp = {}
        ev_tok = {}
        t_sq = {}
        for i in range(NT):
            if i >= 2:
                p.wait("sync", t_x1[i - 2])
            lx = f"ld_x{i % 2}"
            tx = p.dma("sync", xb[i % 2][:], XR[i * 128:(i + 1) * 128, :], sem=lx)
            if i >= 1:
                p.wait("sync", t_x1[i - 1])
                p.wait("sync", t_sq[i - 1])
            ty = p.dma("sync", yb[0][:], YR[i * 128:(i + 1) * 128, :], sem="ld_y0")
            tf = p.dma("sync", fb[0][:], F0[i * 128:(i + 1) * 128, :], sem="ld_f0")
            for tk in (tx, ty, tf):
                p.wait("vector", tk)
            if i >= 1:
                p.wait("vector", t_st[i - 1])
                p.wait("vector", t_cp[i - 1])
            p.op("vector", lambda e, i=i: e.tensor_add(x1f[:], xb[i % 2][:], yb[i % 2][:]))
            t_x1[i] = p.op("vector", lambda e, i=i: e.tensor_add(
                x1f[:], x1f[:], fb[i % 2][:]))
            p.wait("gpsimd", t_x1[i])
            t_st[i] = p.dma("gpsimd", X1[i * 128:(i + 1) * 128, :], x1f[:], sem="st_x1")
            p.op("vector", lambda e: e.reduce_sum(mur[:], x1f[:], axis=AX.X))
            p.op("vector", lambda e: e.tensor_scalar_mul(mu[:], mur[:], 1.0 / D))
            p.wait("scalar", t_x1[i])
            if i >= 1:
                p.wait("scalar", t_cp[i - 1])
            tsq = t_sq[i] = p.op("scalar", lambda e, i=i: e.activation(
                fb[i % 2][:], x1f[:], AF.Square, accum_out=sqa[:]))
            p.op("vector", lambda e: e.tensor_mul(t1[:], mu[:], mu[:]))
            p.wait("vector", tsq)
            p.op("vector", lambda e: e.tensor_scalar_mul(var[:], sqa[:], 1.0 / D))
            tvar = p.op("vector", lambda e: e.tensor_sub(var[:], var[:], t1[:]))
            p.wait("scalar", tvar)
            tsd = p.op("scalar", lambda e: e.activation(sd[:], var[:], AF.Sqrt,
                                                        bias=epsc[:]))
            p.wait("vector", tsd)
            p.op("vector", lambda e: e.reciprocal(rstd[:], sd[:]))
            t_cp[i] = p.op("vector", lambda e, i=i: e.scalar_tensor_tensor(
                x1b[:], x1f[:], mu[:], rstd[:].broadcast_to((128, D)),
                ALU.subtract, ALU.mult))
            p.wait("tensor", t_cp[i])
            p.wait_all("tensor", "ldc")
            for c in range(NCHUNK):
                g = 16 * i + c
                pt = pst[c % 2]
                if g >= 2:
                    p.wait("tensor", ev_tok[g - 2])
                ttp = p.op("tensor", lambda e, pt=pt, c=c: e.transpose(
                    pt[:], x1b[:, c * 128:(c + 1) * 128], ident[:]))
                p.wait("vector", ttp)
                ev_tok[g] = p.op("vector", lambda e, pt=pt, c=c, i=i: e.tensor_copy(
                    xnT[:, c * TOKH + 1 + i * 128: c * TOKH + 1 + (i + 1) * 128], pt[:]))

        p.wait_all("vector", "ld_hb")
        for c in range(NCHUNK):
            p.wait("gpsimd", ev_tok[16 * (NT - 1) + c])
            p.dma("gpsimd", XN1T[c * 128:(c + 1) * 128, :],
                  xnT[:, c * TOKH + 1:(c + 1) * TOKH], sem="st_xnT")

        mix_tok = {}
        last_mm = None
        prev_ev = None
        prev_tso = None
        for si, s in enumerate("rkvg"):
            if last_mm is not None:
                p.wait("gpsimd", last_mm)
            for c in range(NCHUNK):
                p.dma("gpsimd", wsb[:, c * D:(c + 1) * D],
                      WT[s][c * 128:(c + 1) * 128, :], sem=f"ld_w{s}")
            if last_mm is not None:
                p.wait("vector", last_mm)
            for c in range(NCHUNK):
                if si == 0:
                    p.wait("vector", ev_tok[16 * (NT - 1) + c])
                cur = xnT[:, c * TOKH + 1:(c + 1) * TOKH]
                shf = xnT[:, c * TOKH:(c + 1) * TOKH - 1]
                p.op("vector", lambda e, c=c, si=si, cur=cur: e.tensor_single_scalar(
                    xtmp[:], cur, omt[:, si * NCHUNK + c:si * NCHUNK + c + 1], ALU.mult))
                mix_tok[(si, c)] = p.op("vector",
                    lambda e, c=c, si=si, shf=shf: e.scalar_tensor_tensor(
                        xsb[:, c * TOK:(c + 1) * TOK], shf,
                        tmt[:, si * NCHUNK + c:si * NCHUNK + c + 1], xtmp[:],
                        ALU.mult, ALU.add))
            ev_ff = {}
            eng = "scalar" if s == "g" else "vector"
            if prev_tso is not None:
                p.wait(eng, prev_tso)
            for n in range(NCHUNK):
                for j in range(2):
                    m = 2 * n + j
                    pp = psA if m % 2 == 0 else psB
                    if m >= 2:
                        p.wait("tensor", ev_ff[m - 2])
                    elif prev_ev is not None:
                        p.wait("tensor", prev_ev[30 + m])
                    if m == 0:
                        p.wait_all("tensor", f"ld_w{s}")
                        p.wait("tensor", mix_tok[(si, NCHUNK - 1)])
                        p.wait("tensor", mix_tok[(si, NCHUNK - 2)])
                    for c in range(NCHUNK):
                        tmm = p.op("tensor", lambda e, pp=pp, n=n, j=j, c=c: e.matmul(
                            pp[:], wsb[:, c * D + n * 128: c * D + (n + 1) * 128],
                            xsb[:, c * TOK + j * 512: c * TOK + j * 512 + 512],
                            start=(c == 0), stop=(c == NCHUNK - 1)))
                    if s == "g":
                        p.wait("scalar", tmm)
                        ev_ff[m] = p.op("scalar", lambda e, pp=pp, j=j: e.activation(
                            ob[:, j * 512:(j + 1) * 512], pp[:], AF.Sigmoid))
                    else:
                        p.wait("vector", tmm)
                        ev_ff[m] = p.op("vector", lambda e, pp=pp, j=j: e.tensor_copy(
                            ob[:, j * 512:(j + 1) * 512], pp[:]))
                p.wait("gpsimd", ev_ff[2 * n])
                p.wait("gpsimd", ev_ff[2 * n + 1])
                tso = p.dma("gpsimd", OUT[s][n * 128:(n + 1) * 128, :], ob[:],
                            sem=f"st_o{s}")
                p.wait(eng, tso)
            prev_ev = ev_ff
            last_mm = tmm
            prev_tso = tso

        p.wait_all("gpsimd", "st_x1")
        p.wait_all("gpsimd", "st_xnT")
        for s in "rkvg":
            p.wait_all("gpsimd", f"st_o{s}")
        p.realize()
    return nc




def _phase_C_inmaps(inputs, y_rows, f0_rows):
    x2d = np.ascontiguousarray(inputs["L0_attn_x"].reshape(GT, D))
    wT = {s: np.ascontiguousarray(inputs[f"L1_attn_W{s}"].T) for s in "rkvg"}
    tm = {s: np.ascontiguousarray(inputs[f"L1_attn_tm_{s}"].reshape(D, 1))
          for s in "rkvg"}
    idb = np.eye(128, dtype=bf16)
    maps = []
    for c in range(8):
        r0 = c * TOK
        h = r0 + 1 if c in (0, 4) else r0 - 1
        hlx = np.ascontiguousarray(
            np.stack([x2d[h], y_rows[h], f0_rows[h]], axis=0))
        m = {"xr": x2d[r0:r0 + TOK], "yr": np.ascontiguousarray(y_rows[r0:r0 + TOK]),
             "f0": np.ascontiguousarray(f0_rows[r0:r0 + TOK]), "hlx": hlx,
             "identb": idb}
        for s in "rkvg":
            m[f"w{s}T"] = wT[s]
            m[f"tm{s}"] = tm[s]
        maps.append(m)
    return maps


def run_phase_C(inputs, y_rows, f0_rows):
    nc = build_phase_C()
    res = run_bass_kernel_spmd(nc, _phase_C_inmaps(inputs, y_rows, f0_rows),
                               core_ids=list(range(8)))
    x1 = np.concatenate([r["x1"] for r in res.results], axis=0)
    xn1T = np.concatenate([r["xn1T"] for r in res.results], axis=1)
    outs = {s: np.concatenate([r[f"o{s}"] for r in res.results], axis=1)
            for s in "rkvg"}
    return x1, xn1T, outs




LC_D = 16
SBK_D = 512
NSB_D = T // SBK_D
NCL_D = SBK_D // LC_D
LAM_CLAMP = -5.0
LAM_CLAMP32 = -2.5


def build_phase_D(nsb=NSB_D, ncl=NCL_D, noband=False):
    nc = bass.Bass("TRN2", target_bir_lowering=False)
    RI = nc.dram_tensor("ri", [512, T], F32, kind="ExternalInput").ap()
    KI = nc.dram_tensor("ki", [512, T], F32, kind="ExternalInput").ap()
    VI = nc.dram_tensor("vi", [512, T], F32, kind="ExternalInput").ap()
    GI = nc.dram_tensor("gi", [512, T], F32, kind="ExternalInput").ap()
    WDEC = nc.dram_tensor("wdec", [128, 4], F32, kind="ExternalInput").ap()
    TFIR = nc.dram_tensor("tfir", [128, 4], F32, kind="ExternalInput").ap()
    WOT = nc.dram_tensor("woT", [512, D], F32, kind="ExternalInput").ap()
    IDB = nc.dram_tensor("identb", [128, 128], BF16, kind="ExternalInput").ap()
    IDF = nc.dram_tensor("identf", [128, 128], F32, kind="ExternalInput").ap()
    PY1 = nc.dram_tensor("py1", [D, T], F32, kind="ExternalOutput").ap()
    CDR = nc.dram_tensor("cdr", [1, 4 * 5 * SBK_D], F32, kind="Internal").ap()

    KH = SBK_D + 5

    with contextlib.ExitStack() as stack:
        sb = lambda name, shape, dt: stack.enter_context(nc.sbuf_tensor(name, shape, dt))
        ps = lambda name, shape, dt: stack.enter_context(nc.psum_tensor(name, shape, dt))

        r_sb = sb("r_sb", [128, 4 * SBK_D], F32)
        k_sb = sb("k_sb", [128, 4 * KH], F32)
        v_sb = sb("v_sb", [128, 4 * KH], F32)
        g_sb = sb("g_sb", [128, 4 * SBK_D], F32)
        obig = sb("obig", [128, 4 * SBK_D], F32)
        ocor = sb("ocor", [128, 4 * SBK_D], F32)
        ogb = sb("ogb", [128, 4 * SBK_D], BF16)
        cbc = sb("cbc", [128, 5 * SBK_D], F32)
        mstk = sb("mstk", [128, 5 * SBK_D], F32)
        mtmp = sb("mtmp", [128, SBK_D], F32)
        cvrow = sb("cvrow", [1, SBK_D], F32)
        wo_sb = sb("wo_sb", [128, 4 * D], BF16)
        y1b = sb("y1b", [128, SBK_D], F32)
        ident = sb("identd", [128, 128], BF16)
        idf = sb("idfd", [128, 128], F32)
        ones = sb("onesd", [128, 1], F32)
        lam = sb("lam", [128, 4], F32)
        lamc = sb("lamc", [128, 4], F32)
        nlam = sb("nlam", [128, 4], F32)
        nlamc = sb("nlamc", [128, 4], F32)
        l15 = sb("l15", [128, 4], F32)
        w16 = sb("w16", [128, 4], F32)
        tfir = sb("tfirs", [128, 4], F32)
        pd = sb("pd", [128, 16], F32)
        pdc = sb("pdc", [128, 4], F32)
        it16 = sb("it16", [128, 16], I32)
        mki = sb("mki", [16, 64], I32)
        ski = sb("ski", [16, 64], I32)
        it16f = sb("it16f", [128, 16], F32)
        e1 = sb("e1", [128, 64], F32)
        e1c = sb("e1c", [128, 64], F32)
        e3c = sb("e3c", [128, 64], F32)
        e2 = sb("e2", [128, 64], F32)
        e2t = sb("e2t", [16, 512], F32)
        dlc = sb("dlc", [128, 1], F32)
        rs = sb("rs", [128, 64], F32)
        rc = sb("rc", [128, 64], F32)
        kc = sb("kc", [128, 64], F32)
        kb = sb("kb", [128, 64], F32)
        ktk = sb("ktk", [16, 512], F32)
        vtk = sb("vtk", [16, 512], F32)
        kbt = sb("kbt", [16, 512], F32)
        amask = sb("amask", [16, 64], I32)
        azero = sb("azero", [16, 64], F32)
        asb = sb("asb", [16, 64], F32)
        S_sb = sb("S_sbd", [128, 4 * 128], F32)

        psK = ps("psKd", [16, 512], F32)
        psV = ps("psVd", [16, 512], F32)
        psA16 = ps("psA16", [16, 64], F32)
        psO = ps("psOd", [128, 64], F32)
        psS = ps("psSd", [128, 512], F32)
        psY = [ps(f"psYd{i}", [128, 512], F32) for i in range(2)]
        psC = ps("psCd", [1, 512], F32)

        p = Prog(nc, stack)

        p.dma("sync", ident[:], IDB, sem="ldc")
        p.dma("sync", idf[:], IDF, sem="ldc")
        tw = p.dma("sync", lam[:], WDEC, sem="ldc")
        tf = p.dma("sync", tfir[:], TFIR, sem="ldc")
        for c in range(4):
            p.dma("gpsimd", wo_sb[:, c * D:(c + 1) * D],
                  WOT[c * 128:(c + 1) * 128, :], sem="ldwo")
        p.op("vector", lambda e: e.memset(ones[:], 1.0))
        tmsS = p.op("vector", lambda e: e.memset(S_sb[:], 0.0))
        tmsk = p.op("vector", lambda e: e.memset(k_sb[:], 0.0))
        tmsv = p.op("vector", lambda e: e.memset(v_sb[:], 0.0))
        tit = p.op("gpsimd", lambda e: e.iota(it16[:], pattern=[[1, 16]], base=0,
                                              channel_multiplier=0))
        p.wait("vector", tit)
        p.op("vector", lambda e: e.tensor_copy(it16f[:], it16[:]))
        tmk2 = p.op("gpsimd", lambda e: e.iota(mki[:], pattern=[[0, 4], [1, 16]],
                                               base=0, channel_multiplier=0))
        tsq = p.op("gpsimd", lambda e: e.iota(ski[:], pattern=[[0, 4 * 16]],
                                              base=0, channel_multiplier=1))
        p.wait("vector", tmk2)
        p.wait("vector", tsq)
        p.op("vector", lambda e: e.tensor_tensor(
            amask[:], ski[:], mki[:], ALU.is_lt))
        p.op("vector", lambda e: e.memset(azero[:], 0.0))
        p.wait_all("scalar", "ldc")
        p.wait_all("vector", "ldc")
        tl = p.op("scalar", lambda e: e.activation(lam[:], lam[:], AF.Exp))
        p.wait("vector", tl)
        p.op("vector", lambda e: e.tensor_scalar_mul(lam[:], lam[:], -1.0))
        p.op("vector", lambda e: e.tensor_scalar_max(lamc[:], lam[:], LAM_CLAMP))
        p.op("vector", lambda e: e.tensor_scalar_mul(nlam[:], lam[:], -1.0))
        p.op("vector", lambda e: e.tensor_scalar_mul(nlamc[:], lamc[:], -1.0))
        tl2 = p.op("vector", lambda e: e.tensor_scalar_mul(l15[:], lam[:], 15.0))
        p.wait("scalar", tl2)
        p.op("scalar", lambda e: e.activation(w16[:], lam[:], AF.Exp, scale=16.0))
        for j in range(4):
            p.op("scalar", lambda e, j=j: e.activation(
                e1[:, j * 16:(j + 1) * 16], it16f[:], AF.Exp, scale=lam[:, j:j + 1]))
            p.op("scalar", lambda e, j=j: e.activation(
                e1c[:, j * 16:(j + 1) * 16], it16f[:], AF.Exp, scale=lamc[:, j:j + 1]))
            p.op("scalar", lambda e, j=j: e.activation(
                e3c[:, j * 16:(j + 1) * 16], it16f[:], AF.Exp,
                scale=nlamc[:, j:j + 1], bias=nlamc[:, j:j + 1]))
            tse = p.op("scalar", lambda e, j=j: e.activation(
                e2[:, j * 16:(j + 1) * 16], it16f[:], AF.Exp,
                scale=nlam[:, j:j + 1], bias=l15[:, j:j + 1]))
        for d in range(1, 5):
            tdl = p.op("vector", lambda e, d=d: e.memset(dlc[:], float(d)))
            p.wait("scalar", tdl)
            for j in range(4):
                p.op("scalar", lambda e, j=j, d=d: e.activation(
                    pd[:, j * 4 + d - 1:j * 4 + d], lam[:, j:j + 1], AF.Exp,
                    scale=dlc[:]))
                tpe = p.op("scalar", lambda e, j=j, d=d: e.activation(
                    pdc[:, j:j + 1], lamc[:, j:j + 1], AF.Exp, scale=dlc[:]))
                p.wait("vector", tpe)
                p.op("vector", lambda e, j=j, d=d: e.tensor_sub(
                    pd[:, j * 4 + d - 1:j * 4 + d], pd[:, j * 4 + d - 1:j * 4 + d],
                    pdc[:, j:j + 1]))
        p.wait("tensor", tse)
        p.wait_all("tensor", "ldc")
        te2 = {}
        for j in range(4):
            if j > 0:
                p.wait("tensor", te2[j - 1])
            ttp = p.op("tensor", lambda e, j=j: e.transpose(
                psK[0:16, 0:128], e2[:, j * 16:(j + 1) * 16], idf[:]))
            p.wait("vector", ttp)
            te2[j] = p.op("vector", lambda e, j=j: e.tensor_copy(
                e2t[:, j * 128:(j + 1) * 128], psK[0:16, 0:128]))
        p.wait("tensor", te2[3])

        p.wait("vector", ("ser_scalar", p.val("ser_scalar")))

        ld_guard = None
        ld_guard_pe = None
        prev_tkbt = None
        prev_stt = {j: tmsS for j in range(4)}
        prev_oev = None
        prev_aev = None
        prev_kev = None
        prev_vev = None
        prev_wo = None
        for s in range(nsb):
            t0 = s * SBK_D
            if ld_guard is not None:
                p.wait("sync", ld_guard)
                p.wait("sync", ld_guard_pe)
            else:
                p.wait("sync", tmsk)
                p.wait("sync", tmsv)
            ldr = {}
            for j in range(4):
                ldr[j] = p.dma("sync", r_sb[:, j * SBK_D:(j + 1) * SBK_D],
                               RI[j * 128:(j + 1) * 128, t0:t0 + SBK_D], sem="ld_r")
                if s == 0:
                    p.dma("sync", k_sb[:, j * KH + 5:(j + 1) * KH],
                          KI[j * 128:(j + 1) * 128, 0:SBK_D], sem="ld_k")
                    p.dma("sync", v_sb[:, j * KH + 5:(j + 1) * KH],
                          VI[j * 128:(j + 1) * 128, 0:SBK_D], sem="ld_v")
                else:
                    p.dma("sync", k_sb[:, j * KH:(j + 1) * KH],
                          KI[j * 128:(j + 1) * 128, t0 - 5:t0 + SBK_D], sem="ld_k")
                    p.dma("sync", v_sb[:, j * KH:(j + 1) * KH],
                          VI[j * 128:(j + 1) * 128, t0 - 5:t0 + SBK_D], sem="ld_v")
                ldr[j] = p.dma("sync", g_sb[:, j * SBK_D:(j + 1) * SBK_D],
                               GI[j * 128:(j + 1) * 128, t0:t0 + SBK_D], sem="ld_g")
            p.wait_all("vector", "ld_r")
            p.wait_all("vector", "ld_k")
            p.wait_all("vector", "ld_v")
            p.wait_all("vector", "ld_g")

            if noband:
                pass
            tcv_evac = None
            tmm_band = None
            for j in range([] if noband else range(4)) if False else (range(0) if noband else range(4)):
                if tmm_band is not None:
                    p.wait("vector", tmm_band)
                for di in range(5):
                    if di == 0:
                        tm1 = p.op("vector", lambda e, j=j: e.tensor_single_scalar(
                            mtmp[:], r_sb[:, j * SBK_D:(j + 1) * SBK_D],
                            tfir[:, j:j + 1], ALU.mult))
                        ksl = k_sb[:, j * KH + 5:(j + 1) * KH]
                    else:
                        tm1 = p.op("vector", lambda e, j=j, di=di: e.tensor_single_scalar(
                            mtmp[:], r_sb[:, j * SBK_D:(j + 1) * SBK_D],
                            pd[:, j * 4 + di - 1:j * 4 + di], ALU.mult))
                        ksl = k_sb[:, j * KH + 4 - di:j * KH + 4 - di + SBK_D]
                    tm2 = p.op("vector", lambda e, j=j, di=di, ksl=ksl: e.tensor_mul(
                        mstk[:, di * SBK_D:(di + 1) * SBK_D], mtmp[:], ksl))
                for di in range(5):
                    if tcv_evac is not None:
                        p.wait("tensor", tcv_evac)
                    p.wait("tensor", tm2)
                    tmm = p.op("tensor", lambda e, di=di: e.matmul(
                        psC[:], ones[:], mstk[:, di * SBK_D:(di + 1) * SBK_D],
                        start=True, stop=True))
                    p.wait("vector", tmm)
                    tcv_evac = p.op("vector", lambda e: e.tensor_copy(cvrow[:], psC[:]))
                    p.wait("gpsimd", tcv_evac)
                    p.dma("gpsimd", CDR[:, (j * 5 + di) * SBK_D:(j * 5 + di + 1) * SBK_D],
                          cvrow[:], sem="st_cv")
                    p.wait("vector", ("st_cv", p.val("st_cv")))
                    tmm_band = tmm

            p.wait_all("tensor", "ld_k")
            p.wait_all("tensor", "ld_v")
            for cl in range(ncl):
                c16 = cl * 16
                r3 = r_sb.ap().rearrange("p (j t) -> p j t", j=4)[:, :, c16:c16 + 16]
                k3 = k_sb.ap().rearrange("p (j t) -> p j t", j=4)[:, :, 5 + c16:5 + c16 + 16]
                v3 = v_sb.ap().rearrange("p (j t) -> p j t", j=4)[:, :, 5 + c16:5 + c16 + 16]
                e13 = e1.ap().rearrange("p (j t) -> p j t", j=4)
                e1c3 = e1c.ap().rearrange("p (j t) -> p j t", j=4)
                e3c3 = e3c.ap().rearrange("p (j t) -> p j t", j=4)
                e23 = e2.ap().rearrange("p (j t) -> p j t", j=4)
                rs3 = rs.ap().rearrange("p (j t) -> p j t", j=4)
                rc3 = rc.ap().rearrange("p (j t) -> p j t", j=4)
                kc3 = kc.ap().rearrange("p (j t) -> p j t", j=4)
                kb3 = kb.ap().rearrange("p (j t) -> p j t", j=4)
                if prev_oev is not None:
                    p.wait("vector", prev_oev)
                trs = p.op("vector", lambda e, r3=r3, rs3=rs3, e13=e13: e.tensor_tensor(
                    rs3, r3, e13, ALU.mult))
                trc = p.op("vector", lambda e, r3=r3, rc3=rc3, e1c3=e1c3: e.tensor_tensor(
                    rc3, r3, e1c3, ALU.mult))
                tkc = p.op("vector", lambda e, k3=k3, kc3=kc3, e3c3=e3c3: e.tensor_tensor(
                    kc3, k3, e3c3, ALU.mult))
                if prev_kev is not None:
                    p.wait("tensor", prev_kev)
                    p.wait("tensor", prev_vev)
                for j in range(4):
                    tpk = p.op("tensor", lambda e, j=j, c16=c16: e.transpose(
                        psK[0:16, j * 128:(j + 1) * 128],
                        k_sb[:, j * KH + 5 + c16:j * KH + 5 + c16 + 16], idf[:]))
                for j in range(4):
                    tpv = p.op("tensor", lambda e, j=j, c16=c16: e.transpose(
                        psV[0:16, j * 128:(j + 1) * 128],
                        v_sb[:, j * KH + 5 + c16:j * KH + 5 + c16 + 16], idf[:]))
                p.wait("scalar", tpk)
                if prev_tkbt is not None:
                    p.wait("scalar", prev_tkbt)
                prev_kev = p.op("scalar", lambda e: e.copy(ktk[:], psK[:]))
                p.wait("vector", tpv)
                prev_vev = p.op("vector", lambda e: e.tensor_copy(vtk[:], psV[:]))
                if prev_aev is not None:
                    p.wait("tensor", prev_aev)
                p.wait("tensor", tkc)
                p.wait("tensor", trc)
                for j in range(4):
                    tam = p.op("tensor", lambda e, j=j: e.matmul(
                        psA16[:, j * 16:(j + 1) * 16], kc[:, j * 16:(j + 1) * 16],
                        rc[:, j * 16:(j + 1) * 16], start=True, stop=True))
                p.wait("vector", tam)
                prev_aev = p.op("vector", lambda e: e.select(
                    asb[:], amask[:], psA16[:], azero[:], add_drain=True))
                if prev_oev is not None:
                    p.wait("tensor", prev_oev)
                p.wait("tensor", trs)
                p.wait("tensor", prev_aev)
                p.wait("tensor", prev_vev)
                for j in range(4):
                    if prev_stt[j] is not None:
                        p.wait("tensor", prev_stt[j])
                    p.op("tensor", lambda e, j=j: e.matmul(
                        psO[:, j * 16:(j + 1) * 16], S_sb[:, j * 128:(j + 1) * 128],
                        rs[:, j * 16:(j + 1) * 16], start=True, stop=False))
                    tom = p.op("tensor", lambda e, j=j: e.matmul(
                        psO[:, j * 16:(j + 1) * 16], vtk[:, j * 128:(j + 1) * 128],
                        asb[:, j * 16:(j + 1) * 16], start=False, stop=True))
                p.wait("vector", tom)
                ob3 = obig.ap().rearrange("p (j t) -> p j t", j=4)[:, :, c16:c16 + 16]
                prev_oev = p.op("vector", lambda e, ob3=ob3: e.tensor_copy(
                    ob3, psO.ap().rearrange("p (j t) -> p j t", j=4)))
                p.wait("vector", prev_kev)
                tkbt = prev_tkbt = p.op("vector", lambda e: e.tensor_mul(kbt[:], ktk[:], e2t[:]))
                p.wait("tensor", tkbt)
                for j in range(4):
                    if prev_stt[j] is not None:
                        p.wait("tensor", prev_stt[j])
                    tkv = p.op("tensor", lambda e, j=j: e.matmul(
                        psS[:, j * 128:(j + 1) * 128], kbt[:, j * 128:(j + 1) * 128],
                        vtk[:, j * 128:(j + 1) * 128], start=True, stop=True))
                p.wait("vector", tkv)
                for j in range(4):
                    prev_stt[j] = p.op("vector", lambda e, j=j: e.scalar_tensor_tensor(
                        S_sb[:, j * 128:(j + 1) * 128], S_sb[:, j * 128:(j + 1) * 128],
                        w16[:, j:j + 1], psS[:, j * 128:(j + 1) * 128],
                        ALU.mult, ALU.add))

            p.wait_all("sync", "st_cv")
            tcb = {}
            tcrd = None
            for j in range(0 if noband else 4):
                if tcrd is not None:
                    p.wait("sync", tcrd)
                tcb[j] = p.dma("sync", cbc[:],
                               CDR[:, j * 5 * SBK_D:(j + 1) * 5 * SBK_D]
                               .broadcast_to((128, 5 * SBK_D)), sem="ld_cb")
                p.wait("vector", tcb[j])
                vsl = v_sb[:, j * KH + 5:(j + 1) * KH]
                p.op("vector", lambda e, j=j, vsl=vsl: e.tensor_mul(
                    ocor[:, j * SBK_D:(j + 1) * SBK_D], cbc[:, 0:SBK_D], vsl))
                for di in range(1, 5):
                    vsh = v_sb[:, j * KH + 4 - di:j * KH + 4 - di + SBK_D]
                    tcrd = p.op("vector", lambda e, j=j, di=di, vsh=vsh: e.tensor_mul(
                        mtmp[:], cbc[:, di * SBK_D:(di + 1) * SBK_D], vsh))
                    p.op("vector", lambda e, j=j: e.tensor_add(
                        ocor[:, j * SBK_D:(j + 1) * SBK_D],
                        ocor[:, j * SBK_D:(j + 1) * SBK_D], mtmp[:]))
                p.op("vector", lambda e, j=j: e.tensor_add(
                    obig[:, j * SBK_D:(j + 1) * SBK_D],
                    obig[:, j * SBK_D:(j + 1) * SBK_D],
                    ocor[:, j * SBK_D:(j + 1) * SBK_D]))
                tog = p.op("vector", lambda e, j=j: e.tensor_mul(
                    ogb[:, j * SBK_D:(j + 1) * SBK_D],
                    obig[:, j * SBK_D:(j + 1) * SBK_D],
                    g_sb[:, j * SBK_D:(j + 1) * SBK_D]))
            if noband:
                for j in range(4):
                    tog = p.op("vector", lambda e, j=j: e.tensor_mul(
                        ogb[:, j * SBK_D:(j + 1) * SBK_D],
                        obig[:, j * SBK_D:(j + 1) * SBK_D],
                        g_sb[:, j * SBK_D:(j + 1) * SBK_D]))
            ld_guard = tog
            ld_guard_pe = prev_vev
            p.wait_all("tensor", "ldwo")
            p.wait("tensor", tog)
            twev = {}
            for n in range(NCHUNK):
                pp = psY[n % 2]
                if prev_wo is not None and n < 2:
                    p.wait("tensor", prev_wo)
                if n >= 2:
                    p.wait("tensor", twev[n - 2])
                for j in range(4):
                    twm = p.op("tensor", lambda e, n=n, j=j, pp=pp: e.matmul(
                        pp[:], wo_sb[:, j * D + n * 128:j * D + (n + 1) * 128],
                        ogb[:, j * SBK_D:(j + 1) * SBK_D], start=(j == 0),
                        stop=(j == 3)))
                p.wait("vector", twm)
                twev[n] = p.op("vector", lambda e, pp=pp: e.tensor_copy(y1b[:], pp[:]))
                p.wait("gpsimd", twev[n])
                tsy = p.dma("gpsimd", PY1[n * 128:(n + 1) * 128, t0:t0 + SBK_D],
                            y1b[:], sem="st_y1")
                p.wait("vector", tsy)
            prev_wo = twm

        p.wait_all("gpsimd", "st_y1")
        p.realize()
    return nc


def _phase_D_inmaps(inputs, routs):
    idb = np.eye(128, dtype=bf16)
    idf = np.eye(128, dtype=np.float32)
    td = inputs["L1_attn_time_decay"]
    tf = inputs["L1_attn_time_first"]
    maps = []
    for core in range(8):
        b = core // 4
        h0 = 4 * (core % 4)
        r0 = h0 * 128
        m = {"ri": np.ascontiguousarray(routs["r"][r0:r0 + 512, b * T:(b + 1) * T]),
             "ki": np.ascontiguousarray(routs["k"][r0:r0 + 512, b * T:(b + 1) * T]),
             "vi": np.ascontiguousarray(routs["v"][r0:r0 + 512, b * T:(b + 1) * T]),
             "gi": np.ascontiguousarray(routs["g"][r0:r0 + 512, b * T:(b + 1) * T]),
             "wdec": np.ascontiguousarray(td[h0:h0 + 4].T),
             "tfir": np.ascontiguousarray(tf[h0:h0 + 4].T),
             "woT": np.ascontiguousarray(inputs["L1_attn_Wo"].T[r0:r0 + 512, :]),
             "identb": idb, "identf": idf}
        maps.append(m)
    return maps


def run_phase_D(inputs, routs, sum_partials=True):
    nc = build_phase_D()
    res = run_bass_kernel_spmd(nc, _phase_D_inmaps(inputs, routs),
                               core_ids=list(range(8)))
    if sum_partials:
        y1 = np.zeros((2, D, T), np.float32)
        for core in range(8):
            y1[core // 4] += res.results[core]["py1"]
        return y1
    return [r["py1"] for r in res.results]




TSTRIPE = 512
NSTRIPE = GT // TSTRIPE


def build_phase_E(nstripe=NSTRIPE):
    nc = bass.Bass("TRN2", target_bir_lowering=False)
    GWP = nc.dram_tensor("gwp", [2048, 1024], I32, kind="ExternalInput").ap()
    GSC = nc.dram_tensor("gsc", [2048, 64], F32, kind="ExternalInput").ap()
    UWP = nc.dram_tensor("uwp", [2048, 1024], I32, kind="ExternalInput").ap()
    USC = nc.dram_tensor("usc", [2048, 64], F32, kind="ExternalInput").ap()
    DWP = nc.dram_tensor("dwp", [2048, 1024], I32, kind="ExternalInput").ap()
    DSC = nc.dram_tensor("dsc", [2048, 64], F32, kind="ExternalInput").ap()
    XN1T = nc.dram_tensor("xn1T", [D, GT], BF16, kind="ExternalInput").ap()
    IDB = nc.dram_tensor("identb", [128, 128], BF16, kind="ExternalInput").ap()
    PFFN = nc.dram_tensor("pffn", [D, GT], F32, kind="ExternalOutput").ap()
    INTS = nc.dram_tensor("ints", [2048, GT], BF16, kind="Internal").ap()

    with contextlib.ExitStack() as stack:
        sb = lambda name, shape, dt: stack.enter_context(nc.sbuf_tensor(name, shape, dt))
        ps = lambda name, shape, dt: stack.enter_context(nc.psum_tensor(name, shape, dt))

        wbig = sb("wbig", [128, 2 * NCHUNK * D], BF16)
        pw = [sb(f"pwe{i}", [128, 1024], I32) for i in range(2)]
        scl = [sb(f"scle{i}", [128, 64], F32) for i in range(2)]
        rf = sb("rfe", [128, 1024], F32)
        s8 = sb("s8e", [128, 1024], F32)
        ta = sb("tae", [128, 1024], F32)
        tb2 = sb("tbe", [128, 1024], F32)
        t1 = sb("t1e", [128, 1024], F32)
        wq = sb("wqe", [128, D], BF16)
        xstr = [sb(f"xstr{i}", [128, NCHUNK * TSTRIPE], BF16) for i in range(1)]
        gf = sb("gfe", [128, TSTRIPE], F32)
        uf = sb("ufe", [128, TSTRIPE], F32)
        ga = sb("gae", [128, TSTRIPE], F32)
        gb = sb("gbe", [128, TSTRIPE], F32)
        ib = sb("ibe", [128, NCHUNK * TSTRIPE], BF16)
        ffb = sb("ffbe", [128, TSTRIPE], F32)
        ident = sb("idente", [128, 128], BF16)

        pst = [ps(f"pste{i}", [128, 128], BF16) for i in range(2)]
        psG = [ps(f"psGe{i}", [128, 512], F32) for i in range(2)]
        psU = [ps(f"psUe{i}", [128, 512], F32) for i in range(2)]
        psD = [ps(f"psDe{i}", [128, 512], F32) for i in range(2)]

        p = Prog(nc, stack)
        p.dma("sync", ident[:], IDB, sem="ldc")

        pst_guard = {0: None, 1: None}
        deq_guard = {0: None, 1: None}

        def dequant(WP, SC, dst_off, nrow_chunks=16):
            tr_tok = None
            for rt in range(nrow_chunks):
                bi = rt % 2
                if deq_guard[bi] is not None:
                    p.wait("sync", deq_guard[bi])
                tpw = p.dma("sync", pw[bi][:], WP[rt * 128:(rt + 1) * 128, :],
                            sem=f"ld_pw{bi}")
                tsc = p.dma("sync", scl[bi][:], SC[rt * 128:(rt + 1) * 128, :],
                            sem=f"ld_sc{bi}")
                p.wait("vector", tpw)
                p.wait("vector", tsc)
                p.wait("scalar", tpw)
                for half in range(2):
                    sh = 4 * half
                    if sh:
                        p.op("vector", lambda e, bi=bi, sh=sh: e.tensor_scalar(
                            t1.ap().bitcast(I32), pw[bi][:], sh, 7,
                            ALU.logical_shift_right, ALU.bitwise_and))
                        p.op("vector", lambda e, bi=bi, sh=sh: e.tensor_scalar(
                            ta.ap().bitcast(I32), pw[bi][:], sh, 8,
                            ALU.logical_shift_right, ALU.bitwise_and))
                    else:
                        p.op("vector", lambda e, bi=bi: e.tensor_single_scalar(
                            t1.ap().bitcast(I32), pw[bi][:], 7, ALU.bitwise_and))
                        p.op("vector", lambda e, bi=bi: e.tensor_single_scalar(
                            ta.ap().bitcast(I32), pw[bi][:], 8, ALU.bitwise_and))
                    tcv1 = p.op("vector", lambda e: e.tensor_copy(
                        rf[:], t1.ap().bitcast(I32)))
                    tcv2 = p.op("vector", lambda e: e.tensor_copy(
                        s8[:], ta.ap().bitcast(I32)))
                    p.op("vector", lambda e: e.tensor_scalar(
                        ta[:], rf[:], 4.0, 0.0, ALU.subtract, ALU.max))
                    p.op("vector", lambda e: e.tensor_scalar(
                        tb2[:], rf[:], 6.0, 0.0, ALU.subtract, ALU.max))
                    p.op("vector", lambda e: e.tensor_add(t1[:], rf[:], ta[:]))
                    p.op("vector", lambda e: e.scalar_tensor_tensor(
                        t1[:], tb2[:], 2.0, t1[:], ALU.mult, ALU.add))
                    p.op("vector", lambda e: e.tensor_scalar(
                        s8[:], s8[:], -0.125, 0.5, ALU.mult, ALU.add))
                    p.op("vector", lambda e: e.tensor_mul(t1[:], t1[:], s8[:]))
                    twq = p.op("vector", lambda e, bi=bi, half=half: e.tensor_tensor(
                        wq.ap().rearrange("p (a b) -> p a b", b=2)[:, :, half],
                        t1.ap().rearrange("p (a b) -> p a b", b=16),
                        scl[bi].ap().unsqueeze(2).broadcast_to((128, 64, 16)),
                        ALU.mult))
                deq_guard[bi] = twq
                p.wait("tensor", twq)
                for c in range(NCHUNK):
                    g = rt * NCHUNK + c
                    pt = pst[g % 2]
                    if pst_guard[g % 2] is not None:
                        p.wait("tensor", pst_guard[g % 2])
                    ttp = p.op("tensor", lambda e, pt=pt, c=c: e.transpose(
                        pt[:], wq[:, c * 128:(c + 1) * 128], ident[:]))
                    p.wait("vector", ttp)
                    tev = p.op("vector", lambda e, pt=pt, c=c, rt=rt: e.tensor_copy(
                        wbig[:, dst_off + c * 2048 + rt * 128:
                             dst_off + c * 2048 + (rt + 1) * 128], pt[:]))
                    pst_guard[g % 2] = tev
                    tr_tok = tev
            return tr_tok

        p.wait_all("tensor", "ldc")
        tg = dequant(GWP, GSC, 0)
        tu = dequant(UWP, USC, NCHUNK * D)

        c798 = sb("c798", [128, 1], F32)
        p.op("vector", lambda e: e.memset(c798[:], 0.7978845608028654))
        ev_g = {}
        ev_u = {}
        st_int = {}
        ld_x = {}
        for st in range(nstripe):
            if st >= 1:
                p.wait("sync", st_int[st - 1])
            ld_x[st] = p.dma("sync",
                             xstr[0].ap().rearrange("p (c t) -> p c t", t=TSTRIPE),
                             XN1T.rearrange("(c p) t -> p c t", p=128)
                             [:, :, st * TSTRIPE:(st + 1) * TSTRIPE],
                             sem=f"ld_x{st % 2}")
            p.wait("tensor", ld_x[st])
            if st == 0:
                p.wait("tensor", tg)
                p.wait("tensor", tu)
            for n in range(NCHUNK):
                m = st * NCHUNK + n
                pg = psG[n % 2]
                pu = psU[n % 2]
                if m >= 2:
                    p.wait("tensor", ev_u[m - 2])
                for c in range(NCHUNK):
                    p.op("tensor", lambda e, pg=pg, n=n, c=c: e.matmul(
                        pg[:], wbig[:, c * 2048 + n * 128:c * 2048 + (n + 1) * 128],
                        xstr[0][:, c * TSTRIPE:(c + 1) * TSTRIPE],
                        start=(c == 0), stop=(c == NCHUNK - 1)))
                for c in range(NCHUNK):
                    tmu = p.op("tensor", lambda e, pu=pu, n=n, c=c: e.matmul(
                        pu[:], wbig[:, NCHUNK * D + c * 2048 + n * 128:
                                    NCHUNK * D + c * 2048 + (n + 1) * 128],
                        xstr[0][:, c * TSTRIPE:(c + 1) * TSTRIPE],
                        start=(c == 0), stop=(c == NCHUNK - 1)))
                p.wait("vector", tmu)
                tgf = p.op("vector", lambda e, pg=pg: e.tensor_copy(gf[:], pg[:]))
                ev_u[m] = p.op("vector", lambda e, pu=pu: e.tensor_copy(uf[:], pu[:]))
                p.wait("scalar", tgf)
                tsqg = p.op("scalar", lambda e: e.activation(ga[:], gf[:], AF.Square))
                p.wait("vector", tsqg)
                p.op("vector", lambda e: e.tensor_scalar(
                    ga[:], ga[:], 0.044715, 1.0, ALU.mult, ALU.add))
                tgc = p.op("vector", lambda e: e.tensor_mul(ga[:], ga[:], gf[:]))
                p.wait("scalar", tgc)
                tth = p.op("scalar", lambda e: e.activation(
                    gb[:], ga[:], AF.Tanh, scale=c798[:]))
                p.wait("vector", tth)
                p.op("vector", lambda e: e.scalar_tensor_tensor(
                    gb[:], gb[:], 1.0, gf[:], ALU.add, ALU.mult))
                p.op("vector", lambda e, n=n: e.scalar_tensor_tensor(
                    ib[:, n * TSTRIPE:(n + 1) * TSTRIPE], gb[:], 0.5, uf[:],
                    ALU.mult, ALU.mult))
            p.wait("gpsimd", ("ser_vector", p.val("ser_vector")))
            st_int[st] = p.dma("gpsimd",
                               INTS.rearrange("(c p) t -> p c t", p=128)
                               [:, :, st * TSTRIPE:(st + 1) * TSTRIPE],
                               ib.ap().rearrange("p (c t) -> p c t", t=TSTRIPE),
                               sem=f"st_i{st % 2}")

        p.wait("vector", ("ser_tensor", p.val("ser_tensor")))
        twd = dequant(DWP, DSC, 0)

        p.wait_all("sync", "st_i0")
        p.wait_all("sync", "st_i1")
        ev_d = {}
        st_ff = {}
        for st in range(nstripe):
            tld = p.dma("sync", ib.ap().rearrange("p (c t) -> p c t", t=TSTRIPE),
                        INTS.rearrange("(c p) t -> p c t", p=128)
                        [:, :, st * TSTRIPE:(st + 1) * TSTRIPE],
                        sem=f"ld_i{st % 2}")
            p.wait("tensor", tld)
            if st == 0:
                p.wait("tensor", twd)
            for n in range(NCHUNK):
                m = st * NCHUNK + n
                pd_ = psD[n % 2]
                if m >= 2:
                    p.wait("tensor", ev_d[m - 2])
                for c in range(NCHUNK):
                    tmd = p.op("tensor", lambda e, pd_=pd_, n=n, c=c: e.matmul(
                        pd_[:], wbig[:, c * 2048 + n * 128:c * 2048 + (n + 1) * 128],
                        ib[:, c * TSTRIPE:(c + 1) * TSTRIPE],
                        start=(c == 0), stop=(c == NCHUNK - 1)))
                p.wait("vector", tmd)
                ev_d[m] = p.op("vector", lambda e, pd_=pd_: e.tensor_copy(ffb[:], pd_[:]))
                p.wait("gpsimd", ev_d[m])
                tsf = p.dma("gpsimd",
                            PFFN[n * 128:(n + 1) * 128, st * TSTRIPE:(st + 1) * TSTRIPE],
                            ffb[:], sem="st_ff")
                p.wait("vector", tsf)
            st_ff[st] = tsf
            p.wait("sync", ("ser_tensor", p.val("ser_tensor")))

        p.wait_all("gpsimd", "st_ff")
        p.realize()
    return nc


def _phase_E_inmaps(inputs, xn1T):
    idb = np.eye(128, dtype=bf16)
    maps = []
    xn1T = np.ascontiguousarray(xn1T)
    for c in range(8):
        n0 = c * 2048
        m = {"gwp": np.ascontiguousarray(inputs["L1_ffn_gate_W_packed"][n0:n0 + 2048]),
             "gsc": np.ascontiguousarray(inputs["L1_ffn_gate_scales"][n0:n0 + 2048]),
             "uwp": np.ascontiguousarray(inputs["L1_ffn_up_W_packed"][n0:n0 + 2048]),
             "usc": np.ascontiguousarray(inputs["L1_ffn_up_scales"][n0:n0 + 2048]),
             "dwp": np.ascontiguousarray(inputs["L1_ffn_down_W_packed"][:, c * 1024:(c + 1) * 1024]),
             "dsc": np.ascontiguousarray(inputs["L1_ffn_down_scales"][:, c * 64:(c + 1) * 64]),
             "xn1T": xn1T, "identb": idb}
        maps.append(m)
    return maps


def run_phase_E(inputs, xn1T):
    nc = build_phase_E()
    res = run_bass_kernel_spmd(nc, _phase_E_inmaps(inputs, xn1T),
                               core_ids=list(range(8)))
    return [r["pffn"] for r in res.results]




def build_phase_F():
    nc = bass.Bass("TRN2", target_bir_lowering=False)
    X1 = nc.dram_tensor("x1", [TOK, D], F32, kind="ExternalInput").ap()
    PY = [nc.dram_tensor(f"py{i}", [TOK, D], F32, kind="ExternalInput").ap()
          for i in range(4)]
    PF = [nc.dram_tensor(f"pf{i}", [TOK, D], F32, kind="ExternalInput").ap()
          for i in range(8)]
    LNW = nc.dram_tensor("lnw", [1, D], F32, kind="ExternalInput").ap()
    LNB = nc.dram_tensor("lnb", [1, D], F32, kind="ExternalInput").ap()
    OUT = nc.dram_tensor("out", [TOK, D], F32, kind="ExternalOutput").ap()

    NT = TOK // 128

    with contextlib.ExitStack() as stack:
        sb = lambda name, shape, dt: stack.enter_context(nc.sbuf_tensor(name, shape, dt))

        x1b = sb("x1bf", [128, D], F32)
        pyb = [sb(f"pybf{i}", [128, D], F32) for i in range(4)]
        pfb = [sb(f"pfbf{i}", [128, D], F32) for i in range(8)]
        y1t = sb("y1tf", [128, D], F32)
        att = sb("attf", [128, D], F32)
        lnwb = sb("lnwbf", [128, D], F32)
        lnbb = sb("lnbbf", [128, D], F32)
        mu = sb("muf", [128, 1], F32)
        mur = sb("murf", [128, 1], F32)
        sqa = sb("sqaf", [128, 1], F32)
        var = sb("varf", [128, 1], F32)
        sd = sb("sdf", [128, 1], F32)
        rstd = sb("rstdf", [128, 1], F32)
        t1 = sb("t1f", [128, 1], F32)
        epsc = sb("epscf", [128, 1], F32)

        p = Prog(nc, stack)
        p.dma("sync", lnwb[:], LNW.broadcast_to((128, D)), sem="ldc")
        p.dma("sync", lnbb[:], LNB.broadcast_to((128, D)), sem="ldc")
        p.op("vector", lambda e: e.memset(epsc[:], 1e-5))
        p.wait_all("vector", "ldc")

        tdone = {}
        tsq_t = {}
        for i in range(NT):
            sl = slice(i * 128, (i + 1) * 128)
            if i >= 1:
                p.wait("sync", tdone[i - 1])
                p.wait("sync", tsq_t[i - 1])
            toks = [p.dma("sync", x1b[:], X1[sl, :], sem="ld_a")]
            for j in range(4):
                toks.append(p.dma("sync", pyb[j][:], PY[j][sl, :], sem=f"ld_b{j}"))
            for j in range(8):
                toks.append(p.dma("sync", pfb[j][:], PF[j][sl, :], sem=f"ld_c{j}"))
            for tk in toks:
                p.wait("vector", tk)
            p.op("vector", lambda e: e.tensor_add(y1t[:], pyb[0][:], pyb[1][:]))
            p.op("vector", lambda e: e.tensor_add(y1t[:], y1t[:], pyb[2][:]))
            ty1 = p.op("vector", lambda e: e.tensor_add(y1t[:], y1t[:], pyb[3][:]))
            p.op("vector", lambda e: e.reduce_sum(mur[:], y1t[:], axis=AX.X))
            p.op("vector", lambda e: e.tensor_scalar_mul(mu[:], mur[:], 1.0 / D))
            p.wait("scalar", ty1)
            tsq = tsq_t[i] = p.op("scalar", lambda e: e.activation(
                att[:], y1t[:], AF.Square, accum_out=sqa[:]))
            p.op("vector", lambda e: e.tensor_mul(t1[:], mu[:], mu[:]))
            p.wait("vector", tsq)
            p.op("vector", lambda e: e.tensor_scalar_mul(var[:], sqa[:], 1.0 / D))
            tvar = p.op("vector", lambda e: e.tensor_sub(var[:], var[:], t1[:]))
            p.wait("scalar", tvar)
            tsd = p.op("scalar", lambda e: e.activation(sd[:], var[:], AF.Sqrt,
                                                        bias=epsc[:]))
            p.wait("vector", tsd)
            p.op("vector", lambda e: e.reciprocal(rstd[:], sd[:]))
            p.op("vector", lambda e: e.scalar_tensor_tensor(
                att[:], y1t[:], mu[:], rstd[:].broadcast_to((128, D)),
                ALU.subtract, ALU.mult))
            p.op("vector", lambda e: e.tensor_mul(att[:], att[:], lnwb[:]))
            p.op("vector", lambda e: e.tensor_add(att[:], att[:], lnbb[:]))
            p.op("vector", lambda e: e.tensor_add(att[:], att[:], x1b[:]))
            for j in range(8):
                tlast = p.op("vector", lambda e, j=j: e.tensor_add(
                    att[:], att[:], pfb[j][:]))
            p.wait("gpsimd", tlast)
            tst = p.dma("gpsimd", OUT[sl, :], att[:], sem="st_o")
            p.wait("vector", tst)
            p.wait("scalar", tst)
            tdone[i] = tst

        p.wait_all("gpsimd", "st_o")
        p.realize()
    return nc


def kernel_unfused(**inputs):
    inputs = {k: np.asarray(v) for k, v in inputs.items()}
    xn, delta, bc, ffn0 = run_phase_A(inputs)
    y = run_phase_B(inputs, xn, delta, bc)
    y_rows = np.ascontiguousarray(
        np.concatenate([y[0].T, y[1].T], axis=0))
    f0_rows = np.ascontiguousarray(ffn0.T)
    x1, xn1T, outs = run_phase_C(inputs, y_rows, f0_rows)
    py1s = run_phase_D(inputs, outs, sum_partials=False)
    pffn = run_phase_E(inputs, xn1T)
    nc = build_phase_F()
    lnw = np.ascontiguousarray(inputs["L1_attn_ln_w"].reshape(1, D))
    lnb = np.ascontiguousarray(inputs["L1_attn_ln_b"].reshape(1, D))
    py_rows = [np.ascontiguousarray(p_.T) for p_ in py1s]
    pf_rows = [np.ascontiguousarray(pf.T) for pf in pffn]
    maps = []
    for c in range(8):
        b = c // 4
        sl = slice(c * TOK, (c + 1) * TOK)
        tsl = slice((c % 4) * TOK, (c % 4 + 1) * TOK)
        m = {"x1": np.ascontiguousarray(x1[sl]), "lnw": lnw, "lnb": lnb}
        for j in range(4):
            m[f"py{j}"] = np.ascontiguousarray(py_rows[4 * b + j][tsl])
        for j in range(8):
            m[f"pf{j}"] = np.ascontiguousarray(pf_rows[j][sl])
        maps.append(m)
    res = run_bass_kernel_spmd(nc, maps, core_ids=list(range(8)))
    out = np.concatenate([r["out"] for r in res.results], axis=0)
    return out.reshape(B, T, D)



def build_fused(stages=99, dbg=False):
    nc = bass.Bass("TRN2", target_bir_lowering=False)
    XT = nc.dram_tensor("xt", [D, T], F32, kind="ExternalInput").ap()
    XTS = nc.dram_tensor("xts", [512, T], F32, kind="ExternalInput").ap()
    WBCT = nc.dram_tensor("wbcT", [D, 128], F32, kind="ExternalInput").ap()
    WDTS = nc.dram_tensor("wdtTs", [64, 512], F32, kind="ExternalInput").ap()
    BDTS = nc.dram_tensor("bdts", [512, 1], F32, kind="ExternalInput").ap()
    ALOGS = nc.dram_tensor("alogs", [512, 32], F32, kind="ExternalInput").ap()
    DPARS = nc.dram_tensor("dpars", [512, 1], F32, kind="ExternalInput").ap()
    W0T = nc.dram_tensor("w0T", [D, D], F32, kind="ExternalInput").ap()
    W0TS = nc.dram_tensor("w0Ts", [D, 512], F32, kind="ExternalInput").ap()
    B0S = nc.dram_tensor("b0s", [512, 1], F32, kind="ExternalInput").ap()
    WPT = {s: nc.dram_tensor(f"w{s}Ts", [D, 512], F32, kind="ExternalInput").ap()
           for s in "rkvg"}
    TM = {s: nc.dram_tensor(f"tm{s}", [D, 1], F32, kind="ExternalInput").ap()
          for s in "rkvg"}
    WOTS = nc.dram_tensor("woTs", [D, 512], F32, kind="ExternalInput").ap()
    WDEC = nc.dram_tensor("wdec", [128, 4], F32, kind="ExternalInput").ap()
    TFIR = nc.dram_tensor("tfir", [128, 4], F32, kind="ExternalInput").ap()
    LNWS = nc.dram_tensor("lnws", [512, 1], F32, kind="ExternalInput").ap()
    LNBS = nc.dram_tensor("lnbs", [512, 1], F32, kind="ExternalInput").ap()
    GWP = nc.dram_tensor("gwp", [2048, 1024], I32, kind="ExternalInput").ap()
    GSC = nc.dram_tensor("gsc", [2048, 64], F32, kind="ExternalInput").ap()
    UWP = nc.dram_tensor("uwp", [2048, 1024], I32, kind="ExternalInput").ap()
    USC = nc.dram_tensor("usc", [2048, 64], F32, kind="ExternalInput").ap()
    DWP = nc.dram_tensor("dwp", [2048, 1024], I32, kind="ExternalInput").ap()
    DSC = nc.dram_tensor("dsc", [2048, 64], F32, kind="ExternalInput").ap()
    IDB = nc.dram_tensor("identb", [128, 128], BF16, kind="ExternalInput").ap()
    IDF = nc.dram_tensor("identf", [128, 128], F32, kind="ExternalInput").ap()

    OUTR = nc.dram_tensor("outr", [T, 512], F32, kind="ExternalOutput").ap()
    DBG = {}
    if dbg:
        for nm, shp, dt_ in [("d_x1", [512, T], F32), ("d_xn1g", [2048, T], BF16),
                             ("d_r", [512, T], F32), ("d_g", [512, T], F32),
                             ("d_y1", [512, T], F32), ("d_rso0", [512, 512], F32),
                             ("d_int", [2048, 512], BF16), ("d_mu", [4, T], F32),
                             ("d_wdt", [128, NCHUNK * D], BF16),
                             ("d_ib", [128, NCHUNK * TSTRIPE], BF16),
                             ("d_rsi", [4096, 512], F32)]:
            DBG[nm] = nc.dram_tensor(nm, shp, dt_, kind="ExternalOutput").ap()

    XNT_S = nc.dram_tensor("xnt_s", [D, T], BF16, kind="Internal").ap()
    U_S = nc.dram_tensor("u_s", [512, T], F32, kind="Internal").ap()
    DL_S = nc.dram_tensor("dl_s", [512, T], F32, kind="Internal").ap()
    BCT_S = nc.dram_tensor("bct_s", [T, 64], F32, kind="Internal").ap()
    BCN_S = nc.dram_tensor("bcn_s", [64, T], F32, kind="Internal").ap()
    Y_S = nc.dram_tensor("y_s", [512, T], F32, kind="Internal").ap()
    X1_S = nc.dram_tensor("x1_s", [512, T], F32, kind="Internal").ap()
    MU_S = nc.dram_tensor("mu_s", [4, T], F32, kind="Internal").ap()
    AR1I = nc.dram_tensor("ar1i", [2, T], F32, kind="Internal").ap()
    AR1O = nc.dram_tensor("ar1o", [2, T], F32, kind="Internal").ap()
    XN1I = nc.dram_tensor("xn1i", [512, T], BF16, kind="Internal").ap()
    XN1G = nc.dram_tensor("xn1g", [2048, T], BF16, kind="Internal").ap()
    XN1A = nc.dram_tensor("xn1a", [4096, T], BF16, kind="Internal").ap()
    RKVG_S = {s: nc.dram_tensor(f"{s}_sf", [512, T], F32, kind="Internal").ap()
              for s in "rkvg"}
    OGI = nc.dram_tensor("ogi", [512, T], BF16, kind="Internal").ap()
    OGO = nc.dram_tensor("ogo", [2048, T], BF16, kind="Internal").ap()
    Y1_S = nc.dram_tensor("y1_s", [512, T], F32, kind="Internal").ap()
    ARYI = nc.dram_tensor("aryi", [2, T], F32, kind="Internal").ap()
    ARYO = nc.dram_tensor("aryo", [2, T], F32, kind="Internal").ap()
    INT_S = nc.dram_tensor("int_sf", [2048, GT], BF16, kind="Internal").ap()
    RSI = [nc.dram_tensor(f"rsi{s}", [4096, 512], F32, kind="Internal").ap()
           for s in range(8)]
    RSO = [nc.dram_tensor(f"rso{s}", [512, 512], F32, kind="Internal").ap()
           for s in range(8)]
    CDR = nc.dram_tensor("cdrf", [1, 4 * 5 * SBK_D], F32, kind="Internal").ap()
    TOTS = nc.dram_tensor("totsf", [1, 1], F32, kind="Internal").ap()

    G4 = [[0, 1, 2, 3], [4, 5, 6, 7]]
    G8 = [[0, 1, 2, 3, 4, 5, 6, 7]]
    NTT = T // 512
    Q4 = 4

    with contextlib.ExitStack() as stack:
        sb = lambda name, shape, dt: stack.enter_context(nc.sbuf_tensor(name, shape, dt))
        ps = lambda name, shape, dt: stack.enter_context(nc.psum_tensor(name, shape, dt))
        p = Prog(nc, stack)
        _amap = {
            "ld_s1": "H0", "ld_mb": "H1", "st_mu": "S0",
            "ld_s20": "H2", "ld_s21": "H3", "st_u0": "S1", "st_u1": "S2",
            "st_xn0": "S3", "st_xn1": "S4",
            "ldb2": "H4", "ld_xn0": "H2", "ld_xn1": "H3",
            "ldw2": "S1", "st_bc": "S2", "st_dl0": "S3", "st_dl1": "S4",
            "ld_al0": "H5", "ld_al1": "H6", "ld_al2": "H7", "ld_al3": "H8",
            "ld_dp0": "H9", "ld_dp1": "H10", "ld_dp2": "H11", "ld_dp3": "H12",
            "ld_b0": "H13", "ld_b1": "H14", "ld_c0": "H15", "ld_c1": "H4",
            "ld_dl0": "H2", "ld_dl1": "H3", "ld_u0": "H0", "ld_u1": "H1",
            "st_y0": "S1", "st_y1": "S2",
            "ldb4": "H5", "ld_w40": "H6", "ld_w41": "H7", "ld_x40": "H2",
            "ld_x41": "H3", "ld_y40": "H8", "ld_y41": "H9", "ld_xx40": "H10",
            "ld_xx41": "H11", "ld_ar": "H12",
            "st_tot": "S3", "ld_tot": "S4", "st_x10": "S1", "st_x11": "S2",
            "st_ar": "S0",
            "ld_x50": "H2", "ld_x51": "H3", "st_x50": "S1", "st_x51": "S2",
            "ldtm6": "H4", "ld_xg0": "H5", "ld_xg1": "H6",
            "ld_wp0": "S3", "ld_wp1": "S4", "st_xs0": "S5", "st_xs1": "S6",
            "ld_xm0": "S7", "ld_xm1": "S8", "st_p0": "S1", "st_p1": "S2",
            "ld_r": "H2", "ld_k": "H3", "ld_v": "H7", "ld_g": "H8",
            "ld_cb": "H9", "st_cv": "S3", "st_og": "S4",
            "ld_og0": "H2", "ld_og1": "H3", "ld_ar8": "H4",
            "ld_wo8": "S1", "st_y80": "S2", "st_y81": "S3", "st_ar8": "S5",
            "ld_pw0": "H2", "ld_pw1": "H3", "ld_sc0": "H5", "ld_sc1": "H6",
            "ld_x90": "H7", "ld_x91": "H8", "ld_i90": "H9", "ld_i91": "H10",
            "st_i90": "S1", "st_i91": "S2", "st_ff9": "S4",
            "ldyA0": "H2", "ldyA1": "H3", "ldxA0": "H4", "ldxA1": "H5",
            "ldfA0": "H6", "ldfA1": "H7", "ld_lnA": "H8", "st_out": "S1",
        }
        p.alias.update(_amap)

        _nsc = [None]

        def mark(name):
            if _nsc[0] is not None:
                nc.leave_named_scope(_nsc[0][0], _nsc[0][1], False)
                _nsc[0] = None
            if name is not None:
                sid, _ = nc.enter_named_scope(name, False)
                _nsc[0] = (name, sid)

        ident = sb("identfu", [128, 128], BF16)
        idf = sb("idffu", [128, 128], F32)
        ones = sb("onesfu", [128, 1], F32)
        epsc = sb("epscfu", [128, 1], F32)
        stmr = contextlib.ExitStack()
        mub = stmr.enter_context(nc.sbuf_tensor("mubfu", [128, T], F32))
        rsb = stmr.enter_context(nc.sbuf_tensor("rsbfu", [128, T], F32))
        p.dma("sync", ident[:], IDB, sem="ldc")
        p.dma("sync", idf[:], IDF, sem="ldc")
        t_ones = p.op("vector", lambda e: e.memset(ones[:], 1.0))
        t_eps = p.op("vector", lambda e: e.memset(epsc[:], 1e-5))

        bank = [ps(f"bank{i}", [128, 512], F32) for i in range(6)]
        psA = bank[0].ap()
        psB = bank[1].ap()
        psR1 = bank[2].ap()[0:1, :]
        psR2 = bank[3].ap()[0:1, :]
        pst = [bank[4].ap().bitcast(BF16)[:, 0:128],
               bank[5].ap().bitcast(BF16)[:, 0:128]]

        def stats_finish(srow, mu_row_off, scale=1.0 / D):
            p.op("vector", lambda e: e.tensor_scalar_mul(
                srow[:, 0:T], srow[:, 0:T], scale))
            p.op("vector", lambda e: e.tensor_scalar_mul(
                srow[:, T:2 * T], srow[:, T:2 * T], scale))
            tmu2 = p.op("vector", lambda e: e.tensor_mul(
                mub[0:1, 0:T], srow[:, 0:T], srow[:, 0:T]))
            tvar = p.op("vector", lambda e: e.tensor_sub(
                srow[:, T:2 * T], srow[:, T:2 * T], mub[0:1, 0:T]))
            p.wait("scalar", tvar)
            p.wait("scalar", t_eps)
            tsd = p.op("scalar", lambda e: e.activation(
                srow[:, T:2 * T], srow[:, T:2 * T], AF.Sqrt, bias=epsc[0:1, :]))
            p.wait("vector", tsd)
            trc = p.op("vector", lambda e: e.reciprocal(
                srow[:, T:2 * T], srow[:, T:2 * T]))
            p.wait("gpsimd", trc)
            p.dma("gpsimd", MU_S[mu_row_off:mu_row_off + 1, :], srow[:, 0:T],
                  sem="st_mu")
            tst = p.dma("gpsimd", MU_S[mu_row_off + 1:mu_row_off + 2, :],
                        srow[:, T:2 * T], sem="st_mu")
            p.wait("sync", tst)
            p.dma("sync", mub[:], MU_S[mu_row_off:mu_row_off + 1, :]
                  .broadcast_to((128, T)), sem="ld_mb")
            tb = p.dma("sync", rsb[:], MU_S[mu_row_off + 1:mu_row_off + 2, :]
                       .broadcast_to((128, T)), sem="ld_mb")
            p.wait("vector", tb)

        mark("S1")
        with nc.sbuf_tensor("xtl", [128, 512], F32) as xtl, \
             nc.sbuf_tensor("xsq", [128, 512], F32) as xsq, \
             nc.sbuf_tensor("xnbf", [128, 512], BF16) as xnbf, \
             nc.sbuf_tensor("srow", [1, 2 * T], F32) as srow:
            ev2 = None
            sqtok = {}
            mmtok = {}
            for j in range(NTT):
                for c in range(NCHUNK):
                    g = j * NCHUNK + c
                    if g >= 1:
                        p.wait("sync", sqtok[g - 1])
                        p.wait("sync", mmtok[g - 1])
                    tld = p.dma("sync", xtl[:],
                                XT[c * 128:(c + 1) * 128, j * 512:(j + 1) * 512],
                                sem="ld_s1")
                    p.wait("scalar", tld)
                    if g >= 1:
                        p.wait("scalar", tm2)
                    sqtok[g] = p.op("scalar", lambda e: e.activation(
                        xsq[:], xtl[:], AF.Square))
                    p.wait("tensor", sqtok[g])
                    if c == 0:
                        p.wait_all("tensor", "ldc")
                        p.wait("tensor", t_ones)
                        if ev2 is not None:
                            p.wait("tensor", ev2)
                    p.op("tensor", lambda e, c=c: e.matmul(
                        psR1[:], ones[:], xtl[:], start=(c == 0),
                        stop=(c == NCHUNK - 1)))
                    tm2 = mmtok[g] = p.op("tensor", lambda e, c=c: e.matmul(
                        psR2[:], ones[:], xsq[:], start=(c == 0),
                        stop=(c == NCHUNK - 1)))
                p.wait("vector", tm2)
                p.op("vector", lambda e, j=j: e.tensor_copy(
                    srow[:, j * 512:(j + 1) * 512], psR1[:]))
                ev2 = p.op("vector", lambda e, j=j: e.tensor_copy(
                    srow[:, T + j * 512:T + (j + 1) * 512], psR2[:]))
            stats_finish(srow, 0)
            p.barrier()
            if stages < 2:
                stmr.close()
                p.realize()
                return nc

            mark("S2")
            st_t = {}
            tmul_t = {}
            for c in range(NCHUNK + Q4):
                own = c >= NCHUNK
                src = XT[(c % NCHUNK) * 128:((c % NCHUNK) + 1) * 128, :] if not own \
                    else XTS[(c - NCHUNK) * 128:(c - NCHUNK + 1) * 128, :]
                for j in range(NTT):
                    g = c * NTT + j
                    if g >= 1:
                        p.wait("sync", tmul_t[g - 1])
                        p.wait("sync", st_t[g - 1])
                    tld = p.dma("sync", xtl[:],
                                src[:, j * 512:(j + 1) * 512], sem=f"ld_s2{g % 2}")
                    p.wait("vector", tld)
                    p.op("vector", lambda e, j=j: e.tensor_sub(
                        xsq[:], xtl[:], mub[:, j * 512:(j + 1) * 512]))
                    if own:
                        tmul = tmul_t[g] = p.op("vector", lambda e, j=j: e.tensor_mul(
                            xtl[:], xsq[:], rsb[:, j * 512:(j + 1) * 512]))
                        p.wait("gpsimd", tmul)
                        st_t[g] = p.dma(
                            "gpsimd",
                            U_S[(c - NCHUNK) * 128:(c - NCHUNK + 1) * 128,
                                j * 512:(j + 1) * 512], xtl[:], sem=f"st_u{g % 2}")
                    else:
                        tmul = tmul_t[g] = p.op("vector", lambda e, j=j: e.tensor_mul(
                            xnbf[:], xsq[:], rsb[:, j * 512:(j + 1) * 512]))
                        p.wait("gpsimd", tmul)
                        st_t[g] = p.dma(
                            "gpsimd",
                            XNT_S[c * 128:(c + 1) * 128, j * 512:(j + 1) * 512],
                            xnbf[:], sem=f"st_xn{g % 2}")
        p.barrier()

        if stages < 3:
            stmr.close()
            p.realize()
            return nc

        mark("S2b")
        with nc.sbuf_tensor("wbc2", [128, NCHUNK * 128], BF16) as wbc2, \
             nc.sbuf_tensor("wdt2", [64, 512], BF16) as wdt2, \
             nc.sbuf_tensor("bdt2", [128, Q4], F32) as bdt2, \
             nc.sbuf_tensor("xnl2", [128, 512], BF16) as xnl2, \
             nc.sbuf_tensor("dbc2", [128, T], F32) as dbc2, \
             nc.sbuf_tensor("dtr2", [64, T], BF16) as dtr2, \
             nc.sbuf_tensor("spm2", [128, 512], F32) as spm2, \
             nc.sbuf_tensor("aa2", [128, 512], F32) as aa2, \
             nc.sbuf_tensor("dlt2", [128, 512], F32) as dlt2:
            for c in range(NCHUNK):
                p.dma("gpsimd", wbc2[:, c * 128:(c + 1) * 128],
                      WBCT[c * 128:(c + 1) * 128, :], sem="ldw2")
            p.dma("gpsimd", wdt2[:], WDTS, sem="ldw2")
            for c in range(Q4):
                p.dma("sync", bdt2[:, c:c + 1], BDTS[c * 128:(c + 1) * 128, :],
                      sem="ldb2")
            ev_dbc = {}
            xld = {}
            for j in range(NTT):
                pp = psA if j % 2 == 0 else psB
                if j >= 2:
                    p.wait("tensor", ev_dbc[j - 2])
                for c in range(NCHUNK):
                    g = j * NCHUNK + c
                    if g >= 1:
                        p.wait("sync", xld[g - 1])
                    tld = p.dma("sync", xnl2[:],
                                XNT_S[c * 128:(c + 1) * 128, j * 512:(j + 1) * 512],
                                sem=f"ld_xn{g % 2}")
                    p.wait("tensor", tld)
                    if c == 0:
                        p.wait_all("tensor", "ldw2")
                    tmm = p.op("tensor", lambda e, pp=pp, c=c: e.matmul(
                        pp[:], wbc2[:, c * 128:(c + 1) * 128], xnl2[:],
                        start=(c == 0), stop=(c == NCHUNK - 1)))
                    xld[g] = tmm
                p.wait("vector", tmm)
                ev_dbc[j] = p.op("vector", lambda e, pp=pp, j=j: e.tensor_copy(
                    dbc2[:, j * 512:(j + 1) * 512], pp[:]))
            tdtr = p.op("vector", lambda e: e.tensor_copy(dtr2[:], dbc2[0:64, :]))
            p.wait("gpsimd", ("ser_vector", p.val("ser_vector")))
            p.dma("gpsimd", BCN_S, dbc2[64:128, :], sem="st_bc")
            p.wait("tensor", tdtr)
            p.wait_all("tensor", "ldb2")
            sp_v = {}
            sp_a = {}
            sp_dl = {}
            st_dl = {}
            for q in range(Q4):
                for j in range(NTT):
                    m = q * NTT + j
                    pp = psA if m % 2 == 0 else psB
                    if m == 0:
                        p.wait("tensor", ev_dbc[NTT - 1])
                        p.wait("tensor", ev_dbc[NTT - 2])
                    if m >= 2:
                        p.wait("tensor", sp_v[m - 2])
                        p.wait("tensor", sp_a[m - 2])
                    tmm = p.op("tensor", lambda e, pp=pp, q=q, j=j: e.matmul(
                        pp[:], wdt2[:, q * 128:(q + 1) * 128],
                        dtr2[:, j * 512:(j + 1) * 512], start=True, stop=True))
                    p.wait("vector", tmm)
                    if m >= 1:
                        p.wait("vector", st_dl[m - 1])
                    sp_v[m] = p.op("vector", lambda e, pp=pp, q=q: e.tensor_scalar(
                        spm2[:], pp[:], bdt2[:, q:q + 1], 0.0, ALU.add, ALU.max))
                    p.wait("scalar", tmm)
                    p.wait("scalar", sp_v[m])
                    if m >= 2:
                        p.wait("scalar", sp_dl[m - 2])
                    sp_a[m] = p.op("scalar", lambda e, pp=pp, q=q: e.activation(
                        aa2[:], pp[:], AF.Abs, bias=bdt2[:, q:q + 1]))
                    p.op("scalar", lambda e: e.activation(
                        aa2[:], aa2[:], AF.Exp, scale=-1.0))
                    tln = p.op("scalar", lambda e: e.activation(
                        aa2[:], aa2[:], AF.Ln, bias=1.0))
                    p.wait("vector", tln)
                    sp_dl[m] = p.op("vector", lambda e: e.tensor_add(
                        dlt2[:], spm2[:], aa2[:]))
                    p.wait("gpsimd", sp_dl[m])
                    st_dl[m] = p.dma("gpsimd",
                                     DL_S[q * 128:(q + 1) * 128,
                                          j * 512:(j + 1) * 512],
                                     dlt2[:], sem=f"st_dl{m % 2}")
        p.barrier()
        if stages < 4:
            stmr.close()
            p.realize()
            return nc

        mark("S3")
        F3 = TC_B * 32
        with contextlib.ExitStack() as st3:
            s3 = lambda n, sh, dt: st3.enter_context(nc.sbuf_tensor(n, sh, dt))
            dlb3 = [s3("dlb3", [128, TC_B], F32), s3("dlb3b", [128, TC_B], F32)]
            ub3 = [s3("ub3", [128, TC_B], F32), s3("ub3b", [128, TC_B], F32)]
            du3 = s3("du3", [128, TC_B], F32)
            dA3t = [s3("dA3t", [128, F3], F32), s3("dA3tb", [128, F3], F32)]
            dBu3 = s3("dBu3", [128, F3], F32)
            hh3t = s3("hh3", [128, F3], F32)
            bbc3 = [s3("bbc3", [128, F3], F32), s3("bbc3b", [128, F3], F32)]
            cbc3 = [s3("cbc3", [128, F3], F32), s3("cbc3b", [128, F3], F32)]
            yb3 = [s3("yb3", [128, TC_B], F32), s3("yb3b", [128, TC_B], F32)]
            yr3 = s3("yr3", [128, TC_B], F32)
            negA3 = s3("negA3", [128, NB_B * 32], F32)
            dcol3 = s3("dcol3", [128, NB_B], F32)
            state3 = s3("state3", [128, NB_B * 32], F32)
            for b4 in range(NB_B):
                tl = p.dma("sync", negA3[:, b4 * 32:(b4 + 1) * 32],
                           ALOGS[b4 * 128:(b4 + 1) * 128, :], sem=f"ld_al{b4}")
                p.wait("scalar", tl)
                texp = p.op("scalar", lambda e, b4=b4: e.activation(
                    negA3[:, b4 * 32:(b4 + 1) * 32],
                    negA3[:, b4 * 32:(b4 + 1) * 32], AF.Exp))
                p.wait("vector", texp)
                tn = p.op("vector", lambda e, b4=b4: e.tensor_scalar_mul(
                    negA3[:, b4 * 32:(b4 + 1) * 32],
                    negA3[:, b4 * 32:(b4 + 1) * 32], -1.0))
                p.wait("scalar", tn)
                tdp = p.dma("sync", dcol3[:, b4:b4 + 1],
                            DPARS[b4 * 128:(b4 + 1) * 128, :], sem=f"ld_dp{b4}")
                p.wait("vector", tdp)
            p.op("vector", lambda e: e.memset(state3[:], 0.0))

            def bc_src3(off, c):
                sl = BCN_S[off:off + 32, c * TC_B:(c + 1) * TC_B]
                return sl.unsqueeze(0).broadcast_to((128, 32, TC_B))

            def bc_dst3(buf):
                return buf.ap().rearrange("p (n t) -> p n t", t=TC_B)

            tb3 = {0: p.dma("sync", bc_dst3(bbc3[0]), bc_src3(0, 0), sem="ld_b0")}
            tc3 = {0: p.dma("sync", bc_dst3(cbc3[0]), bc_src3(32, 0), sem="ld_c0")}
            dbu_tok = {}
            ym_tok = {}
            yfin = {}
            scan_last = {}
            st_y = {}
            exp_last = {}
            for c in range(NCH_B):
                if c + 1 < NCH_B:
                    if c >= 1:
                        for b4 in range(NB_B):
                            p.wait("sync", dbu_tok[(c - 1, b4)])
                    tb3[c + 1] = p.dma("sync", bc_dst3(bbc3[(c + 1) % 2]),
                                       bc_src3(0, c + 1),
                                       sem=f"ld_b{(c + 1) % 2}")
                    if c >= 1:
                        for b4 in range(NB_B):
                            p.wait("sync", ym_tok[(c - 1, b4)])
                    tc3[c + 1] = p.dma("sync", bc_dst3(cbc3[(c + 1) % 2]),
                                       bc_src3(32, c + 1),
                                       sem=f"ld_c{(c + 1) % 2}")
                for b4 in range(NB_B):
                    it = c * NB_B + b4
                    if it >= 2:
                        p.wait("sync", yfin[it - 2])
                        p.wait("sync", exp_last[it - 2])
                    tdl = p.dma("sync", dlb3[it % 2][:],
                                DL_S[b4 * 128:(b4 + 1) * 128,
                                     c * TC_B:(c + 1) * TC_B], sem=f"ld_dl{it % 2}")
                    tu = p.dma("sync", ub3[it % 2][:],
                               U_S[b4 * 128:(b4 + 1) * 128,
                                   c * TC_B:(c + 1) * TC_B], sem=f"ld_u{it % 2}")
                    p.wait("vector", tdl)
                    p.wait("vector", tu)
                    tdu = p.op("vector", lambda e, it=it: e.tensor_mul(
                        du3[:], dlb3[it % 2][:], ub3[it % 2][:]))
                    p.wait("scalar", tdl)
                    if it >= 2:
                        p.wait("scalar", scan_last[it - 2])
                    for n in range(32):
                        exp_last[it] = p.op("scalar",
                            lambda e, n=n, b4=b4, it=it: e.activation(
                                dA3t[it % 2][:, n * TC_B:(n + 1) * TC_B],
                                dlb3[it % 2][:], AF.Exp,
                                scale=negA3[:, b4 * 32 + n:b4 * 32 + n + 1]))
                    p.wait("vector", tb3[c])
                    dbu_tok[(c, b4)] = p.op("vector", lambda e, c=c: e.tensor_mul(
                        dBu3.ap().rearrange("p (n t) -> p n t", t=TC_B),
                        du3.ap().unsqueeze(1).broadcast_to((128, 32, TC_B)),
                        bbc3[c % 2][:].rearrange("p (n t) -> p n t", t=TC_B)))
                    p.wait("vector", exp_last[it])
                    for n in range(32):
                        scan_last[it] = p.op("vector",
                            lambda e, n=n, b4=b4, it=it: e.tensor_tensor_scan(
                                hh3t[:, n * TC_B:(n + 1) * TC_B],
                                dA3t[it % 2][:, n * TC_B:(n + 1) * TC_B],
                                dBu3[:, n * TC_B:(n + 1) * TC_B],
                                state3[:, b4 * 32 + n:b4 * 32 + n + 1],
                                ALU.mult, ALU.add))
                    p.op("vector", lambda e, b4=b4: e.tensor_copy(
                        state3[:, b4 * 32:(b4 + 1) * 32],
                        hh3t.ap().rearrange("p (n t) -> p n t", t=TC_B)
                        [:, :, TC_B - 1]))
                    p.wait("vector", tc3[c])
                    ym_tok[(c, b4)] = p.op("vector", lambda e, c=c: e.tensor_mul(
                        hh3t[:], hh3t[:], cbc3[c % 2][:]))
                    for half in (16, 8, 4, 2, 1):
                        p.op("vector", lambda e, half=half: e.tensor_add(
                            hh3t[:, 0:half * TC_B], hh3t[:, 0:half * TC_B],
                            hh3t[:, half * TC_B:2 * half * TC_B]))
                    if it >= 2:
                        p.wait("vector", st_y[it - 2])
                    yfin[it] = p.op("vector", lambda e, it=it, b4=b4:
                        e.scalar_tensor_tensor(
                            yb3[it % 2][:], ub3[it % 2][:], dcol3[:, b4:b4 + 1],
                            hh3t[:, 0:TC_B], ALU.mult, ALU.add))
                    p.wait("gpsimd", yfin[it])
                    st_y[it] = p.dma("gpsimd",
                                     Y_S[b4 * 128:(b4 + 1) * 128,
                                         c * TC_B:(c + 1) * TC_B],
                                     yb3[it % 2][:], sem=f"st_y{it % 2}")
        p.barrier()
        if stages < 5:
            stmr.close()
            p.realize()
            return nc

        mark("S2c")
        with contextlib.ExitStack() as st4:
            s4 = lambda n, sh, dt: st4.enter_context(nc.sbuf_tensor(n, sh, dt))
            wld4 = [s4("wld4a", [128, D], F32), s4("wld4b", [128, D], F32)]
            absa4 = s4("absa4", [128, 1], F32)
            abst4 = s4("abst4", [128, 1], F32)
            tot4 = s4("tot4", [1, 1], F32)
            totb4 = s4("totb4", [128, 1], F32)
            thr4 = s4("thr4", [128, 1], F32)
            s0c4 = s4("s0c4", [128, 1], F32)
            awt4 = s4("awt4", [128, 512], F32)
            sgt4 = s4("sgt4", [128, 512], F32)
            get4 = s4("get4", [128, 512], F32)
            w0q4 = s4("w0q4", [128, NCHUNK * 512], BF16)
            b0c4 = s4("b0c4", [128, Q4], F32)
            xnl4 = s4("xnl4", [128, 512], BF16)
            ytl4 = s4("ytl4", [128, 512], F32)
            xtl4 = s4("xtl4", [128, 512], F32)
            x1t4 = s4("x1t4", [128, 512], F32)
            x1sq4 = s4("x1sq4", [128, 512], F32)
            srow4 = s4("srow4", [1, 2 * T], F32)
            ptot4 = bank[4].ap()[0:1, 0:1]

            for c in range(Q4):
                p.dma("sync", b0c4[:, c:c + 1], B0S[c * 128:(c + 1) * 128, :],
                      sem="ldb4")
            wu4 = {}
            for c in range(NCHUNK):
                wb = wld4[c % 2]
                if c >= 2:
                    p.wait("sync", wu4[c - 2])
                lw = p.dma("sync", wb[:], W0T[c * 128:(c + 1) * 128, :],
                           sem=f"ld_w4{c % 2}")
                p.wait("vector", lw)
                if c == 0:
                    wu4[c] = p.op("vector", lambda e, wb=wb: e.reduce_sum(
                        absa4[:], wb[:], axis=AX.X, apply_absolute_value=True))
                else:
                    p.op("vector", lambda e, wb=wb: e.reduce_sum(
                        abst4[:], wb[:], axis=AX.X, apply_absolute_value=True))
                    wu4[c] = p.op("vector", lambda e: e.tensor_add(
                        absa4[:], absa4[:], abst4[:]))
            p.wait("tensor", wu4[NCHUNK - 1])
            tmt = p.op("tensor", lambda e: e.matmul(ptot4[:], ones[:], absa4[:],
                                                    start=True, stop=True))
            p.wait("vector", tmt)
            tev = p.op("vector", lambda e: e.tensor_copy(tot4[:], ptot4[:]))
            p.wait("gpsimd", tev)
            tst = p.dma("gpsimd", TOTS, tot4[:], sem="st_tot")
            p.wait("gpsimd", tst)
            tld = p.dma("gpsimd", totb4[:], TOTS.broadcast_to((128, 1)), sem="ld_tot")
            p.wait("vector", tld)
            p.op("vector", lambda e: e.tensor_scalar_mul(thr4[:], totb4[:],
                                                         0.5 / (D * D)))
            ts0 = p.op("vector", lambda e: e.tensor_scalar_mul(s0c4[:], totb4[:],
                                                               1.0 / (D * D)))
            q_sg = {}
            q_ge = {}
            q_wq = {}
            for c in range(NCHUNK):
                wb = wld4[c % 2]
                if c >= 2:
                    p.wait("sync", q_sg[c - 2])
                else:
                    p.wait("sync", wu4[NCHUNK - 2 + c])
                lw = p.dma("sync", wb[:, 0:512], W0TS[c * 128:(c + 1) * 128, :],
                           sem=f"ld_w4{c % 2}")
                p.wait("scalar", lw)
                if c >= 1:
                    p.wait("scalar", q_ge[c - 1])
                    p.wait("scalar", q_wq[c - 1])
                p.op("scalar", lambda e, wb=wb: e.activation(
                    awt4[:], wb[:, 0:512], AF.Abs))
                q_sg[c] = p.op("scalar", lambda e, wb=wb: e.activation(
                    sgt4[:], wb[:, 0:512], AF.Sign))
                p.wait("vector", q_sg[c])
                q_ge[c] = p.op("vector", lambda e: e.tensor_single_scalar(
                    get4[:], awt4[:], thr4[:], ALU.is_ge))
                q_wq[c] = p.op("vector", lambda e, c=c: e.tensor_mul(
                    w0q4[:, c * 512:(c + 1) * 512], get4[:], sgt4[:]))
            p.wait_all("vector", "ldb4")
            p.wait("tensor", q_wq[NCHUNK - 1])
            ev_ff = {}
            ldg = {}
            for q in range(Q4):
                for j in range(NTT):
                    m = q * NTT + j
                    pp = psA if m % 2 == 0 else psB
                    if m >= 2:
                        p.wait("tensor", ev_ff[m - 2])
                    for c in range(NCHUNK):
                        g = m * NCHUNK + c
                        if g >= 1:
                            p.wait("sync", ldg[g - 1])
                        tldx = p.dma("sync", xnl4[:],
                                     XNT_S[c * 128:(c + 1) * 128,
                                           j * 512:(j + 1) * 512],
                                     sem=f"ld_x4{g % 2}")
                        p.wait("tensor", tldx)
                        ldg[g] = p.op("tensor", lambda e, pp=pp, q=q, c=c: e.matmul(
                            pp[:], w0q4[:, c * 512 + q * 128:c * 512 + (q + 1) * 128],
                            xnl4[:], start=(c == 0), stop=(c == NCHUNK - 1)))
                    tmm = ldg[m * NCHUNK + NCHUNK - 1]
                    tly = p.dma("sync", ytl4[:],
                                Y_S[q * 128:(q + 1) * 128, j * 512:(j + 1) * 512],
                                sem=f"ld_y4{m % 2}")
                    tlx = p.dma("sync", xtl4[:],
                                XTS[q * 128:(q + 1) * 128, j * 512:(j + 1) * 512],
                                sem=f"ld_xx4{m % 2}")
                    p.wait("vector", tmm)
                    p.wait("vector", tly)
                    p.wait("vector", tlx)
                    if m >= 1:
                        p.wait("vector", stx1[m - 1])
                        p.wait("vector", mm1t[m - 1])
                        p.wait("vector", mm2t[m - 1])
                    p.op("vector", lambda e, pp=pp, q=q: e.scalar_tensor_tensor(
                        x1t4[:], pp[:], s0c4[:],
                        b0c4[:, q:q + 1].broadcast_to((128, 512)),
                        ALU.mult, ALU.add))
                    p.op("vector", lambda e: e.tensor_add(x1t4[:], x1t4[:], ytl4[:]))
                    ev_ff[m] = tx1 = p.op("vector", lambda e: e.tensor_add(
                        x1t4[:], x1t4[:], xtl4[:]))
                    p.wait("gpsimd", tx1)
                    if "stx1" not in dir():
                        stx1 = {}
                        mm1t = {}
                        mm2t = {}
                    stx1[m] = p.dma("gpsimd",
                                    X1_S[q * 128:(q + 1) * 128,
                                         j * 512:(j + 1) * 512],
                                    x1t4[:], sem=f"st_x1{m % 2}")
                    p.wait("scalar", tx1)
                    tsq = p.op("scalar", lambda e: e.activation(
                        x1sq4[:], x1t4[:], AF.Square))
                    p.wait("tensor", tx1)
                    p.wait("tensor", tsq)
                    mm1 = p.op("tensor", lambda e: e.matmul(
                        psR1[:], ones[:], x1t4[:], start=True, stop=True))
                    mm2 = p.op("tensor", lambda e: e.matmul(
                        psR2[:], ones[:], x1sq4[:], start=True, stop=True))
                    p.wait("vector", mm1)
                    p.wait("vector", mm2)
                    if q == 0:
                        mm1t[m] = p.op("vector", lambda e, j=j: e.tensor_copy(
                            srow4[:, j * 512:(j + 1) * 512], psR1[:]))
                        mm2t[m] = p.op("vector", lambda e, j=j: e.tensor_copy(
                            srow4[:, T + j * 512:T + (j + 1) * 512], psR2[:]))
                    else:
                        mm1t[m] = p.op("vector", lambda e, j=j: e.tensor_tensor(
                            srow4[:, j * 512:(j + 1) * 512],
                            srow4[:, j * 512:(j + 1) * 512],
                            psR1[:], ALU.add))
                        mm2t[m] = p.op("vector", lambda e, j=j: e.tensor_tensor(
                            srow4[:, T + j * 512:T + (j + 1) * 512],
                            srow4[:, T + j * 512:T + (j + 1) * 512],
                            psR2[:], ALU.add))
                    p.wait("scalar", mm2t[m])
            tcp = mm2t[Q4 * NTT - 1]
            p.wait("gpsimd", tcp)
            p.dma("gpsimd", AR1I[0:1, :], srow4[:, 0:T], sem="st_ar")
            tsa = p.dma("gpsimd", AR1I[1:2, :], srow4[:, T:2 * T], sem="st_ar")
            p.wait("gpsimd", tsa)
            tar = p.op("gpsimd", lambda e: e.collective_compute(
                "AllReduce", ALU.add, ins=[AR1I.opt()], outs=[AR1O.opt()],
                replica_groups=G4))
            p.wait("sync", tar)
            tld1 = p.dma("sync", srow4[:, 0:T], AR1O[0:1, :], sem="ld_ar")
            tld2 = p.dma("sync", srow4[:, T:2 * T], AR1O[1:2, :], sem="ld_ar")
            p.wait("vector", tld2)
            stats_finish(srow4, 2)
        p.barrier()
        if stages < 6:
            stmr.close()
            p.realize()
            return nc

        mark("S4b")
        with contextlib.ExitStack() as st5:
            s5 = lambda n, sh, dt: st5.enter_context(nc.sbuf_tensor(n, sh, dt))
            xtl5 = s5("xtl5", [128, T], F32)
            xnb5 = s5("xnb5", [128, T], BF16)
            st5t = {}
            tm5 = {}
            for c in range(Q4):
                if c >= 1:
                    p.wait("sync", tm5[c - 1])
                    p.wait("sync", st5t[c - 1])
                tld = p.dma("sync", xtl5[:], X1_S[c * 128:(c + 1) * 128, :],
                            sem=f"ld_x5{c % 2}")
                p.wait("vector", tld)
                p.op("vector", lambda e: e.tensor_sub(xtl5[:], xtl5[:], mub[:]))
                tmu = tm5[c] = p.op("vector", lambda e: e.tensor_mul(
                    xnb5[:], xtl5[:], rsb[:]))
                p.wait("gpsimd", tmu)
                st5t[c] = p.dma("gpsimd", XN1I[c * 128:(c + 1) * 128, :], xnb5[:],
                                sem=f"st_x5{c % 2}")
            p.wait_all("gpsimd", "st_x50")
            p.wait_all("gpsimd", "st_x51")
            tag1 = p.op("gpsimd", lambda e: e.collective_compute(
                "AllGather", ALU.bypass, ins=[XN1I.opt()], outs=[XN1G.opt()],
                replica_groups=G4))
            tag2 = p.op("gpsimd", lambda e: e.collective_compute(
                "AllGather", ALU.bypass, ins=[XN1I.opt()], outs=[XN1A.opt()],
                replica_groups=G8))
        p.barrier()
        if stages < 7:
            stmr.close()
            p.realize()
            return nc

        mark("S5")
        with contextlib.ExitStack() as st6:
            s6 = lambda n, sh, dt: st6.enter_context(nc.sbuf_tensor(n, sh, dt))
            wsb6 = [s6("wsb6a", [128, NCHUNK * 512], BF16),
                    s6("wsb6b", [128, NCHUNK * 512], BF16)]
            tmc6 = s6("tmc6", [128, 4 * NCHUNK], F32)
            omc6 = s6("omc6", [128, 4 * NCHUNK], F32)
            xn6 = s6("xn6", [128, T], BF16)
            xs6 = s6("xs6", [128, T], BF16)
            ot6 = s6("ot6", [128, T], F32)
            xj6 = s6("xj6", [128, NCHUNK * 512], BF16)
            ob6 = s6("ob6", [128, 512], F32)
            for si, s in enumerate("rkvg"):
                for c in range(NCHUNK):
                    p.dma("sync", tmc6[:, si * NCHUNK + c:si * NCHUNK + c + 1],
                          TM[s][c * 128:(c + 1) * 128, :], sem="ldtm6")
            p.wait_all("vector", "ldtm6")
            p.op("vector", lambda e: e.tensor_scalar(
                omc6[:], tmc6[:], -1.0, 1.0, ALU.mult, ALU.add))
            for c in range(NCHUNK):
                p.dma("gpsimd", wsb6[0][:, c * 512:(c + 1) * 512],
                      WPT["r"][c * 128:(c + 1) * 128, :], sem="ld_wp0")
            prev_mm = None
            mixt = {}
            mixv = {}
            mmld = {}
            sto = {}
            ev_p = {}
            gmm = 0
            gmix = 0
            for si, s in enumerate("rkvg"):
                wcur = wsb6[si % 2]
                if si + 1 < 4:
                    nxt = "rkvg"[si + 1]
                    if prev_mm is not None:
                        p.wait("gpsimd", prev_mm)
                    for c in range(NCHUNK):
                        p.dma("gpsimd", wsb6[(si + 1) % 2][:, c * 512:(c + 1) * 512],
                              WPT[nxt][c * 128:(c + 1) * 128, :],
                              sem=f"ld_wp{(si + 1) % 2}")
                if si > 0:
                    p.wait_all("gpsimd", "ld_xm0")
                    p.wait_all("gpsimd", "ld_xm1")
                for c in range(NCHUNK):
                    gmix += 1
                    if gmix - 1 in mixv:
                        p.wait("sync", mixv[gmix - 1])
                        p.wait("sync", mixt[gmix - 1])
                    tldx = p.dma("sync", xn6[:], XN1G[c * 128:(c + 1) * 128, :],
                                 sem=f"ld_xg{gmix % 2}")
                    p.wait("vector", tldx)
                    if gmix - 2 in mixt:
                        p.wait("vector", mixt[gmix - 2])
                    p.op("vector", lambda e, si=si, c=c: e.tensor_single_scalar(
                        ot6[:], xn6[:],
                        omc6[:, si * NCHUNK + c:si * NCHUNK + c + 1], ALU.mult))
                    p.op("vector", lambda e, si=si, c=c: e.scalar_tensor_tensor(
                        xs6[:, 1:T], xn6[:, 0:T - 1],
                        tmc6[:, si * NCHUNK + c:si * NCHUNK + c + 1],
                        ot6[:, 1:T], ALU.mult, ALU.add))
                    tmx = mixv[gmix] = p.op("vector", lambda e, si=si, c=c:
                        e.scalar_tensor_tensor(
                            xs6[:, 0:1], xn6[:, 1:2],
                            tmc6[:, si * NCHUNK + c:si * NCHUNK + c + 1],
                            ot6[:, 0:1], ALU.mult, ALU.add))
                    p.wait("gpsimd", tmx)
                    mixt[gmix] = p.dma("gpsimd", XNT_S[c * 128:(c + 1) * 128, :],
                                       xs6[:], sem=f"st_xs{gmix % 2}")
                    p.wait("vector", mixt[gmix])
                last_tmm5 = None
                for j in range(NTT):
                    if last_tmm5 is not None:
                        p.wait("gpsimd", last_tmm5)
                    for c in range(NCHUNK):
                        gmm += 1
                        tldm = p.dma("gpsimd", xj6[:, c * 512:(c + 1) * 512],
                                     XNT_S[c * 128:(c + 1) * 128,
                                           j * 512:(j + 1) * 512],
                                     sem=f"ld_xm{gmm % 2}")
                        if j == 0 and c == 0:
                            p.wait("gpsimd", mixt[gmix])
                            p.wait("gpsimd", mixt[gmix - 1])
                    p.wait_all("tensor", f"ld_xm{gmm % 2}")
                    p.wait_all("tensor", f"ld_xm{(gmm - 1) % 2}")
                    for q in range(Q4):
                        m = si * Q4 * NTT + j * Q4 + q
                        pp = psA if m % 2 == 0 else psB
                        if m >= 2:
                            p.wait("tensor", ev_p[m - 2])
                        if m == si * Q4 * NTT:
                            p.wait_all("tensor", f"ld_wp{si % 2}")
                        for c in range(NCHUNK):
                            tmm = p.op("tensor", lambda e, pp=pp, q=q, c=c,
                                       wcur=wcur: e.matmul(
                                pp[:], wcur[:, c * 512 + q * 128:
                                            c * 512 + (q + 1) * 128],
                                xj6[:, c * 512:(c + 1) * 512],
                                start=(c == 0), stop=(c == NCHUNK - 1)))
                        last_tmm5 = tmm
                        if s == "g":
                            p.wait("scalar", tmm)
                            if m >= 1:
                                p.wait("scalar", sto[m - 1])
                            ev_p[m] = p.op("scalar", lambda e, pp=pp: e.activation(
                                ob6[:], pp[:], AF.Sigmoid))
                            p.wait("gpsimd", ev_p[m])
                        else:
                            p.wait("vector", tmm)
                            if m >= 1:
                                p.wait("vector", sto[m - 1])
                            ev_p[m] = p.op("vector", lambda e, pp=pp: e.tensor_copy(
                                ob6[:], pp[:]))
                            p.wait("gpsimd", ev_p[m])
                        sto[m] = p.dma("gpsimd",
                                       RKVG_S[s][q * 128:(q + 1) * 128,
                                                 j * 512:(j + 1) * 512],
                                       ob6[:], sem=f"st_p{m % 2}")
                prev_mm = tmm
                p.wait("vector", sto[max(sto)])
                p.wait("scalar", sto[max(sto)])
        p.barrier()
        if stages < 8:
            stmr.close()
            p.realize()
            return nc

        mark("S6")
        psK6 = bank[2].ap()[0:32, :]
        psV6 = bank[3].ap()[0:32, :]
        psS6 = bank[0].ap()
        psO6 = bank[1].ap()[:, 0:128]
        psA16 = bank[4].ap()[0:32, 0:128]
        psC6 = bank[5].ap()[0:1, :]
        KH = SBK_D + 5
        with contextlib.ExitStack() as st7:
            s7 = lambda n, sh, dt: st7.enter_context(nc.sbuf_tensor(n, sh, dt))
            r_sb = s7("r_sb7", [128, 4 * SBK_D], F32)
            k_sb = s7("k_sb7", [128, 4 * KH], F32)
            v_sb = s7("v_sb7", [128, 4 * KH], F32)
            g_sb = s7("g_sb7", [128, 4 * SBK_D], F32)
            obig = s7("obig7", [128, 4 * SBK_D], F32)
            ocor = s7("ocor7", [128, 4 * SBK_D], F32)
            ogb = s7("ogb7", [128, 4 * SBK_D], BF16)
            cbc = s7("cbc7", [128, 5 * SBK_D], F32)
            mstk = s7("mstk7", [128, 5 * SBK_D], F32)
            mtmp = s7("mtmp7", [128, SBK_D], F32)
            cvrow = s7("cvrow7", [1, SBK_D], F32)
            lam = s7("lam7", [128, 4], F32)
            lamc = s7("lamc7", [128, 4], F32)
            nlam = s7("nlam7", [128, 4], F32)
            nlamc = s7("nlamc7", [128, 4], F32)
            l15 = s7("l157", [128, 4], F32)
            w16 = s7("w167", [128, 4], F32)
            tfir7 = s7("tfir7", [128, 4], F32)
            pd7 = s7("pd7", [128, 16], F32)
            pdc7 = s7("pdc7", [128, 4], F32)
            it16 = s7("it167", [128, 32], I32)
            it16f = s7("it16f7", [128, 32], F32)
            mki = s7("mki7", [32, 128], I32)
            ski = s7("ski7", [32, 128], I32)
            e1 = s7("e17", [128, 128], F32)
            e1c = s7("e1c7", [128, 128], F32)
            e3c = s7("e3c7", [128, 128], F32)
            e2 = s7("e27", [128, 128], F32)
            e2t = s7("e2t7", [32, 512], F32)
            dlc = s7("dlc7", [128, 1], F32)
            rs7 = s7("rs7", [128, 128], F32)
            rc7 = s7("rc7", [128, 128], F32)
            kc7 = s7("kc7", [128, 128], F32)
            ktk = s7("ktk7", [32, 512], F32)
            vtk = s7("vtk7", [32, 512], F32)
            kbt = s7("kbt7", [32, 512], F32)
            amask = s7("amask7", [32, 128], I32)
            azero = s7("azero7", [32, 128], F32)
            asb = s7("asb7", [32, 128], F32)
            S_sb = s7("S_sb7", [128, 4 * 128], F32)
            imod = s7("imod7", [1, 512], I32)
            cmski = s7("cmski7", [1, 512], I32)
            cmsk = s7("cmsk7", [1, 4 * 512], F32)

            tw = p.dma("sync", lam[:], WDEC, sem="ldc")
            tf7 = p.dma("sync", tfir7[:], TFIR, sem="ldc")
            tmsS = p.op("vector", lambda e: e.memset(S_sb[:], 0.0))
            tmsk = p.op("vector", lambda e: e.memset(k_sb[:], 0.0))
            tmsv = p.op("vector", lambda e: e.memset(v_sb[:], 0.0))
            tit = p.op("gpsimd", lambda e: e.iota(it16[:], pattern=[[1, 32]], base=0,
                                                  channel_multiplier=0))
            p.wait("vector", tit)
            p.op("vector", lambda e: e.tensor_copy(it16f[:], it16[:]))
            tmk2 = p.op("gpsimd", lambda e: e.iota(mki[:], pattern=[[0, 4], [1, 32]],
                                                   base=0, channel_multiplier=0))
            tsq7 = p.op("gpsimd", lambda e: e.iota(ski[:], pattern=[[0, 4 * 32]],
                                                   base=0, channel_multiplier=1))
            p.wait("vector", tmk2)
            p.wait("vector", tsq7)
            p.op("vector", lambda e: e.tensor_tensor(
                amask[:], ski[:], mki[:], ALU.is_lt))
            p.op("vector", lambda e: e.memset(azero[:], 0.0))
            tio6 = p.op("gpsimd", lambda e: e.iota(
                imod[:], pattern=[[1, 512]], base=0, channel_multiplier=0))
            p.wait("vector", tio6)
            p.op("vector", lambda e: e.tensor_single_scalar(
                imod[:], imod[:], 31, ALU.bitwise_and))
            for d in range(1, 5):
                p.op("vector", lambda e, d=d: e.tensor_single_scalar(
                    cmski[:], imod[:], d + 1, ALU.is_ge))
                p.op("vector", lambda e, d=d: e.tensor_copy(
                    cmsk[0:1, (d - 1) * SBK_D:d * SBK_D], cmski[:]))
            p.wait_all("scalar", "ldc")
            p.wait_all("vector", "ldc")
            tl = p.op("scalar", lambda e: e.activation(lam[:], lam[:], AF.Exp))
            p.wait("vector", tl)
            p.op("vector", lambda e: e.tensor_scalar_mul(lam[:], lam[:], -1.0))
            p.op("vector", lambda e: e.tensor_scalar_max(lamc[:], lam[:], LAM_CLAMP32))
            p.op("vector", lambda e: e.tensor_scalar_mul(nlam[:], lam[:], -1.0))
            p.op("vector", lambda e: e.tensor_scalar_mul(nlamc[:], lamc[:], -1.0))
            tl2 = p.op("vector", lambda e: e.tensor_scalar_mul(l15[:], lam[:], 31.0))
            p.wait("scalar", tl2)
            p.op("scalar", lambda e: e.activation(w16[:], lam[:], AF.Exp, scale=32.0))
            for j in range(4):
                p.op("scalar", lambda e, j=j: e.activation(
                    e1[:, j * 32:(j + 1) * 32], it16f[:], AF.Exp,
                    scale=lam[:, j:j + 1]))
                p.op("scalar", lambda e, j=j: e.activation(
                    e1c[:, j * 32:(j + 1) * 32], it16f[:], AF.Exp,
                    scale=lamc[:, j:j + 1]))
                p.op("scalar", lambda e, j=j: e.activation(
                    e3c[:, j * 32:(j + 1) * 32], it16f[:], AF.Exp,
                    scale=nlamc[:, j:j + 1], bias=nlamc[:, j:j + 1]))
                tse = p.op("scalar", lambda e, j=j: e.activation(
                    e2[:, j * 32:(j + 1) * 32], it16f[:], AF.Exp,
                    scale=nlam[:, j:j + 1], bias=l15[:, j:j + 1]))
            for d in range(1, 5):
                tdl = p.op("vector", lambda e, d=d: e.memset(dlc[:], float(d)))
                p.wait("scalar", tdl)
                for j in range(4):
                    p.op("scalar", lambda e, j=j, d=d: e.activation(
                        pd7[:, j * 4 + d - 1:j * 4 + d], lam[:, j:j + 1], AF.Exp,
                        scale=dlc[:]))
                    tpe = p.op("scalar", lambda e, j=j, d=d: e.activation(
                        pdc7[:, j:j + 1], lamc[:, j:j + 1], AF.Exp, scale=dlc[:]))
                    p.wait("vector", tpe)
                    p.op("vector", lambda e, j=j, d=d: e.tensor_sub(
                        pd7[:, j * 4 + d - 1:j * 4 + d],
                        pd7[:, j * 4 + d - 1:j * 4 + d], pdc7[:, j:j + 1]))
            p.wait("tensor", tse)
            te2 = {}
            for j in range(4):
                if j > 0:
                    p.wait("tensor", te2[j - 1])
                ttp = p.op("tensor", lambda e, j=j: e.transpose(
                    psK6[0:32, 0:128], e2[:, j * 32:(j + 1) * 32], idf[:]))
                p.wait("vector", ttp)
                te2[j] = p.op("vector", lambda e, j=j: e.tensor_copy(
                    e2t[:, j * 128:(j + 1) * 128], psK6[0:32, 0:128]))
            p.wait("tensor", te2[3])
            p.wait("vector", ("ser_scalar", p.val("ser_scalar")))

            ld_guard = None
            ld_guard_pe = None
            prev_tkbt = None
            prev_stt = {j: tmsS for j in range(4)}
            prev_oev = None
            prev_aev = None
            prev_kev = None
            prev_vev = None
            st_og = {}
            for sbx in range(NSB_D):
                t0 = sbx * SBK_D
                if ld_guard is not None:
                    p.wait("sync", ld_guard)
                    p.wait("sync", ld_guard_pe)
                else:
                    p.wait("sync", tmsk)
                    p.wait("sync", tmsv)
                for j in range(4):
                    p.dma("sync", r_sb[:, j * SBK_D:(j + 1) * SBK_D],
                          RKVG_S["r"][j * 128:(j + 1) * 128, t0:t0 + SBK_D],
                          sem="ld_r")
                    if sbx == 0:
                        p.dma("sync", k_sb[:, j * KH + 5:(j + 1) * KH],
                              RKVG_S["k"][j * 128:(j + 1) * 128, 0:SBK_D], sem="ld_k")
                        p.dma("sync", v_sb[:, j * KH + 5:(j + 1) * KH],
                              RKVG_S["v"][j * 128:(j + 1) * 128, 0:SBK_D], sem="ld_v")
                    else:
                        p.dma("sync", k_sb[:, j * KH:(j + 1) * KH],
                              RKVG_S["k"][j * 128:(j + 1) * 128, t0 - 5:t0 + SBK_D],
                              sem="ld_k")
                        p.dma("sync", v_sb[:, j * KH:(j + 1) * KH],
                              RKVG_S["v"][j * 128:(j + 1) * 128, t0 - 5:t0 + SBK_D],
                              sem="ld_v")
                    p.dma("sync", g_sb[:, j * SBK_D:(j + 1) * SBK_D],
                          RKVG_S["g"][j * 128:(j + 1) * 128, t0:t0 + SBK_D],
                          sem="ld_g")
                p.wait_all("vector", "ld_r")
                p.wait_all("vector", "ld_k")
                p.wait_all("vector", "ld_v")
                p.wait_all("vector", "ld_g")
                tcv_evac = None
                tmm_band = None
                for j in range(4):
                    if tmm_band is not None:
                        p.wait("vector", tmm_band)
                    for di in range(5):
                        if di == 0:
                            p.op("vector", lambda e, j=j: e.tensor_single_scalar(
                                mtmp[:], r_sb[:, j * SBK_D:(j + 1) * SBK_D],
                                tfir7[:, j:j + 1], ALU.mult))
                            ksl = k_sb[:, j * KH + 5:(j + 1) * KH]
                        else:
                            p.op("vector", lambda e, j=j, di=di: e.tensor_single_scalar(
                                mtmp[:], r_sb[:, j * SBK_D:(j + 1) * SBK_D],
                                pd7[:, j * 4 + di - 1:j * 4 + di], ALU.mult))
                            ksl = k_sb[:, j * KH + 4 - di:j * KH + 4 - di + SBK_D]
                        tm2b = p.op("vector", lambda e, j=j, di=di, ksl=ksl:
                            e.tensor_mul(mstk[:, di * SBK_D:(di + 1) * SBK_D],
                                         mtmp[:], ksl))
                    for di in range(5):
                        if tcv_evac is not None:
                            p.wait("tensor", tcv_evac)
                        p.wait("tensor", tm2b)
                        tmmb = p.op("tensor", lambda e, di=di: e.matmul(
                            psC6[:], ones[:], mstk[:, di * SBK_D:(di + 1) * SBK_D],
                            start=True, stop=True))
                        p.wait("vector", tmmb)
                        tcv_evac = p.op("vector", lambda e: e.tensor_copy(
                            cvrow[:], psC6[:]))
                        if di > 0:
                            tcv_evac = p.op("vector", lambda e, di=di: e.tensor_mul(
                                cvrow[:], cvrow[:],
                                cmsk[0:1, (di - 1) * SBK_D:di * SBK_D]))
                        p.wait("gpsimd", tcv_evac)
                        p.dma("gpsimd",
                              CDR[:, (j * 5 + di) * SBK_D:(j * 5 + di + 1) * SBK_D],
                              cvrow[:], sem="st_cv")
                        p.wait("vector", ("st_cv", p.val("st_cv")))
                        tmm_band = tmmb
                p.wait_all("tensor", "ld_k")
                p.wait_all("tensor", "ld_v")
                for cl in range(SBK_D // 32):
                    c16 = cl * 32
                    r3 = r_sb.ap().rearrange("p (j t) -> p j t", j=4)[:, :, c16:c16 + 32]
                    k3 = k_sb.ap().rearrange("p (j t) -> p j t", j=4)[:, :, 5 + c16:5 + c16 + 32]
                    e13 = e1.ap().rearrange("p (j t) -> p j t", j=4)
                    e1c3 = e1c.ap().rearrange("p (j t) -> p j t", j=4)
                    e3c3 = e3c.ap().rearrange("p (j t) -> p j t", j=4)
                    rs3 = rs7.ap().rearrange("p (j t) -> p j t", j=4)
                    rc3 = rc7.ap().rearrange("p (j t) -> p j t", j=4)
                    kc3 = kc7.ap().rearrange("p (j t) -> p j t", j=4)
                    if prev_oev is not None:
                        p.wait("vector", prev_oev)
                    trs = p.op("vector", lambda e, r3=r3, rs3=rs3, e13=e13:
                        e.tensor_tensor(rs3, r3, e13, ALU.mult))
                    trc = p.op("vector", lambda e, r3=r3, rc3=rc3, e1c3=e1c3:
                        e.tensor_tensor(rc3, r3, e1c3, ALU.mult))
                    tkc = p.op("vector", lambda e, k3=k3, kc3=kc3, e3c3=e3c3:
                        e.tensor_tensor(kc3, k3, e3c3, ALU.mult))
                    if prev_kev is not None:
                        p.wait("tensor", prev_kev)
                        p.wait("tensor", prev_vev)
                    for j in range(4):
                        tpk = p.op("tensor", lambda e, j=j, c16=c16: e.transpose(
                            psK6[0:32, j * 128:(j + 1) * 128],
                            k_sb[:, j * KH + 5 + c16:j * KH + 5 + c16 + 32], idf[:]))
                    for j in range(4):
                        tpv = p.op("tensor", lambda e, j=j, c16=c16: e.transpose(
                            psV6[0:32, j * 128:(j + 1) * 128],
                            v_sb[:, j * KH + 5 + c16:j * KH + 5 + c16 + 32], idf[:]))
                    p.wait("scalar", tpk)
                    if prev_tkbt is not None:
                        p.wait("scalar", prev_tkbt)
                    prev_kev = p.op("scalar", lambda e: e.copy(ktk[:], psK6[:]))
                    p.wait("vector", tpv)
                    prev_vev = p.op("vector", lambda e: e.tensor_copy(vtk[:], psV6[:]))
                    if prev_aev is not None:
                        p.wait("tensor", prev_aev)
                    p.wait("tensor", tkc)
                    p.wait("tensor", trc)
                    for j in range(4):
                        tam = p.op("tensor", lambda e, j=j: e.matmul(
                            psA16[:, j * 32:(j + 1) * 32], kc7[:, j * 32:(j + 1) * 32],
                            rc7[:, j * 32:(j + 1) * 32], start=True, stop=True))
                    p.wait("vector", tam)
                    prev_aev = p.op("vector", lambda e: e.select(
                        asb[:], amask[:], psA16[:], azero[:], add_drain=True))
                    if prev_oev is not None:
                        p.wait("tensor", prev_oev)
                    p.wait("tensor", trs)
                    p.wait("tensor", prev_aev)
                    p.wait("tensor", prev_vev)
                    for j in range(4):
                        if prev_stt[j] is not None:
                            p.wait("tensor", prev_stt[j])
                        p.op("tensor", lambda e, j=j: e.matmul(
                            psO6[:, j * 32:(j + 1) * 32],
                            S_sb[:, j * 128:(j + 1) * 128],
                            rs7[:, j * 32:(j + 1) * 32], start=True, stop=False))
                        tom = p.op("tensor", lambda e, j=j: e.matmul(
                            psO6[:, j * 32:(j + 1) * 32],
                            vtk[:, j * 128:(j + 1) * 128],
                            asb[:, j * 32:(j + 1) * 32], start=False, stop=True))
                    p.wait("vector", tom)
                    ob3 = obig.ap().rearrange("p (j t) -> p j t", j=4)[:, :, c16:c16 + 32]
                    prev_oev = p.op("vector", lambda e, ob3=ob3: e.tensor_copy(
                        ob3, psO6.rearrange("p (j t) -> p j t", j=4)))
                    p.wait("vector", prev_kev)
                    tkbt = prev_tkbt = p.op("vector", lambda e: e.tensor_mul(
                        kbt[:], ktk[:], e2t[:]))
                    p.wait("tensor", tkbt)
                    for j in range(4):
                        if prev_stt[j] is not None:
                            p.wait("tensor", prev_stt[j])
                        tkv = p.op("tensor", lambda e, j=j: e.matmul(
                            psS6[:, j * 128:(j + 1) * 128],
                            kbt[:, j * 128:(j + 1) * 128],
                            vtk[:, j * 128:(j + 1) * 128], start=True, stop=True))
                    p.wait("vector", tkv)
                    for j in range(4):
                        prev_stt[j] = p.op("vector", lambda e, j=j:
                            e.scalar_tensor_tensor(
                                S_sb[:, j * 128:(j + 1) * 128],
                                S_sb[:, j * 128:(j + 1) * 128],
                                w16[:, j:j + 1], psS6[:, j * 128:(j + 1) * 128],
                                ALU.mult, ALU.add))
                p.wait_all("sync", "st_cv")
                p.wait_all("vector", "st_og")
                tcrd = None
                for j in range(4):
                    if tcrd is not None:
                        p.wait("sync", tcrd)
                    tcb = p.dma("sync", cbc[:],
                                CDR[:, j * 5 * SBK_D:(j + 1) * 5 * SBK_D]
                                .broadcast_to((128, 5 * SBK_D)), sem="ld_cb")
                    p.wait("vector", tcb)
                    vsl = v_sb[:, j * KH + 5:(j + 1) * KH]
                    p.op("vector", lambda e, j=j, vsl=vsl: e.tensor_mul(
                        ocor[:, j * SBK_D:(j + 1) * SBK_D], cbc[:, 0:SBK_D], vsl))
                    for di in range(1, 5):
                        vsh = v_sb[:, j * KH + 4 - di:j * KH + 4 - di + SBK_D]
                        tcrd = p.op("vector", lambda e, j=j, di=di, vsh=vsh:
                            e.tensor_mul(mtmp[:], cbc[:, di * SBK_D:(di + 1) * SBK_D],
                                         vsh))
                        p.op("vector", lambda e, j=j: e.tensor_add(
                            ocor[:, j * SBK_D:(j + 1) * SBK_D],
                            ocor[:, j * SBK_D:(j + 1) * SBK_D], mtmp[:]))
                    p.op("vector", lambda e, j=j: e.tensor_add(
                        obig[:, j * SBK_D:(j + 1) * SBK_D],
                        obig[:, j * SBK_D:(j + 1) * SBK_D],
                        ocor[:, j * SBK_D:(j + 1) * SBK_D]))
                    tog = p.op("vector", lambda e, j=j: e.tensor_mul(
                        ogb[:, j * SBK_D:(j + 1) * SBK_D],
                        obig[:, j * SBK_D:(j + 1) * SBK_D],
                        g_sb[:, j * SBK_D:(j + 1) * SBK_D]))
                ld_guard = tog
                ld_guard_pe = prev_vev
                p.wait("gpsimd", tog)
                for j in range(4):
                    st_og[(sbx, j)] = p.dma("gpsimd",
                                            OGI[j * 128:(j + 1) * 128, t0:t0 + SBK_D],
                                            ogb[:, j * SBK_D:(j + 1) * SBK_D],
                                            sem="st_og")
            p.wait_all("gpsimd", "st_og")
            tago = p.op("gpsimd", lambda e: e.collective_compute(
                "AllGather", ALU.bypass, ins=[OGI.opt()], outs=[OGO.opt()],
                replica_groups=G4))
        p.barrier()
        if stages < 9:
            stmr.close()
            p.realize()
            return nc

        mark("S7")
        with contextlib.ExitStack() as st8:
            s8 = lambda n, sh, dt: st8.enter_context(nc.sbuf_tensor(n, sh, dt))
            wo8 = s8("wo8", [128, NCHUNK * 512], BF16)
            og8 = s8("og8", [128, NCHUNK * 512], BF16)
            y18 = s8("y18", [128, 512], F32)
            y1sq8 = s8("y1sq8", [128, 512], F32)
            srow8 = s8("srow8", [1, 2 * T], F32)
            for c in range(NCHUNK):
                p.dma("gpsimd", wo8[:, c * 512:(c + 1) * 512],
                      WOTS[c * 128:(c + 1) * 128, :], sem="ld_wo8")
            ev8 = {}
            sty8 = {}
            s1t = {}
            s2t = {}
            last_mm8 = None
            for j in range(NTT):
                if last_mm8 is not None:
                    p.wait("sync", last_mm8)
                for c in range(NCHUNK):
                    g = j * NCHUNK + c
                    tld = p.dma("sync", og8[:, c * 512:(c + 1) * 512],
                                OGO[c * 128:(c + 1) * 128, j * 512:(j + 1) * 512],
                                sem=f"ld_og{g % 2}")
                p.wait_all("tensor", f"ld_og{g % 2}")
                p.wait_all("tensor", f"ld_og{(g - 1) % 2}")
                for q in range(Q4):
                    m = j * Q4 + q
                    pp = psA if m % 2 == 0 else psB
                    if m >= 2:
                        p.wait("tensor", ev8[m - 2])
                    if m == 0:
                        p.wait_all("tensor", "ld_wo8")
                    for c in range(NCHUNK):
                        tmm = p.op("tensor", lambda e, pp=pp, q=q, c=c: e.matmul(
                            pp[:], wo8[:, c * 512 + q * 128:c * 512 + (q + 1) * 128],
                            og8[:, c * 512:(c + 1) * 512], start=(c == 0),
                            stop=(c == NCHUNK - 1)))
                    last_mm8 = tmm
                    p.wait("vector", tmm)
                    if m >= 1:
                        p.wait("vector", sty8[m - 1])
                        p.wait("vector", s1t.get(m - 1))
                        p.wait("vector", s2t.get(m - 1))
                    tev = p.op("vector", lambda e, pp=pp: e.tensor_copy(y18[:], pp[:]))
                    ev8[m] = tev
                    p.wait("gpsimd", tev)
                    sty8[m] = p.dma("gpsimd",
                                    Y1_S[q * 128:(q + 1) * 128,
                                         j * 512:(j + 1) * 512], y18[:],
                                    sem=f"st_y8{m % 2}")
                    p.wait("scalar", tev)
                    tsq8 = p.op("scalar", lambda e: e.activation(
                        y1sq8[:], y18[:], AF.Square))
                    p.wait("tensor", tev)
                    p.wait("tensor", tsq8)
                    mm1 = p.op("tensor", lambda e: e.matmul(
                        psR1[:], ones[:], y18[:], start=True, stop=True))
                    mm2 = p.op("tensor", lambda e: e.matmul(
                        psR2[:], ones[:], y1sq8[:], start=True, stop=True))
                    p.wait("vector", mm1)
                    p.wait("vector", mm2)
                    if q == 0:
                        s1t[m] = p.op("vector", lambda e, j=j: e.tensor_copy(
                            srow8[:, j * 512:(j + 1) * 512], psR1[:]))
                        s2t[m] = p.op("vector", lambda e, j=j: e.tensor_copy(
                            srow8[:, T + j * 512:T + (j + 1) * 512], psR2[:]))
                    else:
                        s1t[m] = p.op("vector", lambda e, j=j: e.tensor_tensor(
                            srow8[:, j * 512:(j + 1) * 512],
                            srow8[:, j * 512:(j + 1) * 512], psR1[:], ALU.add))
                        s2t[m] = p.op("vector", lambda e, j=j: e.tensor_tensor(
                            srow8[:, T + j * 512:T + (j + 1) * 512],
                            srow8[:, T + j * 512:T + (j + 1) * 512], psR2[:],
                            ALU.add))
                    p.wait("scalar", s2t[m])
            p.wait("gpsimd", s2t[max(s2t)])
            p.dma("gpsimd", ARYI[0:1, :], srow8[:, 0:T], sem="st_ar8")
            tsa = p.dma("gpsimd", ARYI[1:2, :], srow8[:, T:2 * T], sem="st_ar8")
            p.wait("gpsimd", tsa)
            tar = p.op("gpsimd", lambda e: e.collective_compute(
                "AllReduce", ALU.add, ins=[ARYI.opt()], outs=[ARYO.opt()],
                replica_groups=G4))
            p.wait("sync", tar)
            p.dma("sync", srow8[:, 0:T], ARYO[0:1, :], sem="ld_ar8")
            tld2 = p.dma("sync", srow8[:, T:2 * T], ARYO[1:2, :], sem="ld_ar8")
            p.wait("vector", tld2)
            stats_finish(srow8, 0, scale=1.0 / D)
        p.barrier()
        if stages < 10:
            stmr.close()
            p.realize()
            return nc

        stmr.close()
        mark("S8")
        pstd = [bank[4].ap().bitcast(BF16)[:, 0:128],
                bank[5].ap().bitcast(BF16)[:, 0:128]]
        psG8 = [bank[0].ap(), bank[1].ap()]
        psU8 = [bank[2].ap(), bank[3].ap()]
        with contextlib.ExitStack() as st9:
            s9 = lambda n, sh, dt: st9.enter_context(nc.sbuf_tensor(n, sh, dt))
            wbig = s9("wbig9", [128, 2 * NCHUNK * D], BF16)
            pw = [s9("pw9a", [128, 1024], I32), s9("pw9b", [128, 1024], I32)]
            scl = [s9("scl9a", [128, 64], F32), s9("scl9b", [128, 64], F32)]
            rf = s9("rf9", [128, 1024], F32)
            s8_ = s9("s89", [128, 1024], F32)
            ta9 = s9("ta9", [128, 1024], F32)
            tb9 = s9("tb9", [128, 1024], F32)
            t19 = s9("t19", [128, 1024], F32)
            wq9 = s9("wq9", [128, D], BF16)
            xstr = s9("xstr9", [128, NCHUNK * TSTRIPE], BF16)
            gf9 = s9("gf9", [128, TSTRIPE], F32)
            uf9 = s9("uf9", [128, TSTRIPE], F32)
            ga9 = s9("ga9", [128, TSTRIPE], F32)
            gb9 = s9("gb9", [128, TSTRIPE], F32)
            ib9 = s9("ib9", [128, NCHUNK * TSTRIPE], BF16)
            ffb9 = s9("ffb9", [128, TSTRIPE], F32)
            c798 = s9("c7989", [128, 1], F32)
            p.op("vector", lambda e: e.memset(c798[:], 0.7978845608028654))

            pst_guard = {0: None, 1: None}
            deq_guard = {0: None, 1: None}

            def dequant9(WP, SC, dst_off):
                tr_tok = None
                for rt in range(16):
                    bi = rt % 2
                    if deq_guard[bi] is not None:
                        p.wait("sync", deq_guard[bi])
                    tpw = p.dma("sync", pw[bi][:], WP[rt * 128:(rt + 1) * 128, :],
                                sem=f"ld_pw{bi}")
                    tsc = p.dma("sync", scl[bi][:], SC[rt * 128:(rt + 1) * 128, :],
                                sem=f"ld_sc{bi}")
                    p.wait("vector", tpw)
                    p.wait("vector", tsc)
                    for half in range(2):
                        sh = 4 * half
                        if sh:
                            p.op("vector", lambda e, bi=bi, sh=sh: e.tensor_scalar(
                                t19.ap().bitcast(I32), pw[bi][:], sh, 7,
                                ALU.logical_shift_right, ALU.bitwise_and))
                            p.op("vector", lambda e, bi=bi, sh=sh: e.tensor_scalar(
                                ta9.ap().bitcast(I32), pw[bi][:], sh, 8,
                                ALU.logical_shift_right, ALU.bitwise_and))
                        else:
                            p.op("vector", lambda e, bi=bi: e.tensor_single_scalar(
                                t19.ap().bitcast(I32), pw[bi][:], 7, ALU.bitwise_and))
                            p.op("vector", lambda e, bi=bi: e.tensor_single_scalar(
                                ta9.ap().bitcast(I32), pw[bi][:], 8, ALU.bitwise_and))
                        p.op("vector", lambda e: e.tensor_copy(
                            rf[:], t19.ap().bitcast(I32)))
                        p.op("vector", lambda e: e.tensor_copy(
                            s8_[:], ta9.ap().bitcast(I32)))
                        p.op("vector", lambda e: e.tensor_scalar(
                            ta9[:], rf[:], 4.0, 0.0, ALU.subtract, ALU.max))
                        p.op("vector", lambda e: e.tensor_scalar(
                            tb9[:], rf[:], 6.0, 0.0, ALU.subtract, ALU.max))
                        p.op("vector", lambda e: e.tensor_add(t19[:], rf[:], ta9[:]))
                        p.op("vector", lambda e: e.scalar_tensor_tensor(
                            t19[:], tb9[:], 2.0, t19[:], ALU.mult, ALU.add))
                        p.op("vector", lambda e: e.tensor_scalar(
                            s8_[:], s8_[:], -0.125, 0.5, ALU.mult, ALU.add))
                        p.op("vector", lambda e: e.tensor_mul(t19[:], t19[:], s8_[:]))
                        twq = p.op("vector", lambda e, bi=bi, half=half: e.tensor_tensor(
                            wq9.ap().rearrange("p (a b) -> p a b", b=2)[:, :, half],
                            t19.ap().rearrange("p (a b) -> p a b", b=16),
                            scl[bi].ap().unsqueeze(2).broadcast_to((128, 64, 16)),
                            ALU.mult))
                    deq_guard[bi] = twq
                    p.wait("tensor", twq)
                    for c in range(NCHUNK):
                        g = rt * NCHUNK + c
                        pt = pstd[g % 2]
                        if pst_guard[g % 2] is not None:
                            p.wait("tensor", pst_guard[g % 2])
                        ttp = p.op("tensor", lambda e, pt=pt, c=c: e.transpose(
                            pt[:], wq9[:, c * 128:(c + 1) * 128], ident[:]))
                        p.wait("vector", ttp)
                        tev = p.op("vector", lambda e, pt=pt, c=c, rt=rt: e.tensor_copy(
                            wbig[:, dst_off + c * 2048 + rt * 128:
                                 dst_off + c * 2048 + (rt + 1) * 128], pt[:]))
                        pst_guard[g % 2] = tev
                        tr_tok = tev
                return tr_tok

            tg = dequant9(GWP, GSC, 0)
            tu = dequant9(UWP, USC, NCHUNK * D)

            ev_u = {}
            st_int = {}
            ld_x = {}
            rs_done = {}
            for st in range(NSTRIPE):
                bb, sj = st % 2, st // 2
                if st >= 1:
                    p.wait("sync", st_int[st - 1])
                ld_x[st] = p.dma("sync",
                                 xstr.ap().rearrange("p (c t) -> p c t", t=TSTRIPE),
                                 XN1A.rearrange("(c p) t -> p c t", p=128)
                                 [:, bb * NCHUNK:(bb + 1) * NCHUNK,
                                  sj * TSTRIPE:(sj + 1) * TSTRIPE],
                                 sem=f"ld_x9{st % 2}")
                p.wait("tensor", ld_x[st])
                if st == 0:
                    p.wait("tensor", tg)
                    p.wait("tensor", tu)
                for n in range(NCHUNK):
                    m = st * NCHUNK + n
                    pg = psG8[n % 2]
                    pu = psU8[n % 2]
                    if m >= 2:
                        p.wait("tensor", ev_u[m - 2])
                    for c in range(NCHUNK):
                        p.op("tensor", lambda e, pg=pg, n=n, c=c: e.matmul(
                            pg[:], wbig[:, c * 2048 + n * 128:c * 2048 + (n + 1) * 128],
                            xstr[:, c * TSTRIPE:(c + 1) * TSTRIPE],
                            start=(c == 0), stop=(c == NCHUNK - 1)))
                    for c in range(NCHUNK):
                        tmu = p.op("tensor", lambda e, pu=pu, n=n, c=c: e.matmul(
                            pu[:], wbig[:, NCHUNK * D + c * 2048 + n * 128:
                                        NCHUNK * D + c * 2048 + (n + 1) * 128],
                            xstr[:, c * TSTRIPE:(c + 1) * TSTRIPE],
                            start=(c == 0), stop=(c == NCHUNK - 1)))
                    p.wait("vector", tmu)
                    tgf = p.op("vector", lambda e, pg=pg: e.tensor_copy(gf9[:], pg[:]))
                    ev_u[m] = p.op("vector", lambda e, pu=pu: e.tensor_copy(
                        uf9[:], pu[:]))
                    p.wait("scalar", tgf)
                    tsqg = p.op("scalar", lambda e: e.activation(
                        ga9[:], gf9[:], AF.Square))
                    p.wait("vector", tsqg)
                    p.op("vector", lambda e: e.tensor_scalar(
                        ga9[:], ga9[:], 0.044715, 1.0, ALU.mult, ALU.add))
                    tgc = p.op("vector", lambda e: e.tensor_mul(ga9[:], ga9[:], gf9[:]))
                    p.wait("scalar", tgc)
                    tth = p.op("scalar", lambda e: e.activation(
                        gb9[:], ga9[:], AF.Tanh, scale=c798[:]))
                    p.wait("vector", tth)
                    p.op("vector", lambda e: e.scalar_tensor_tensor(
                        gb9[:], gb9[:], 1.0, gf9[:], ALU.add, ALU.mult))
                    p.op("vector", lambda e, n=n: e.scalar_tensor_tensor(
                        ib9[:, n * TSTRIPE:(n + 1) * TSTRIPE], gb9[:], 0.5, uf9[:],
                        ALU.mult, ALU.mult))
                p.wait("gpsimd", ("ser_vector", p.val("ser_vector")))
                st_int[st] = p.dma("gpsimd",
                                   INT_S.rearrange("(c p) t -> p c t", p=128)
                                   [:, :, (bb * NTT + sj) * TSTRIPE:
                                    (bb * NTT + sj + 1) * TSTRIPE],
                                   ib9.ap().rearrange("p (c t) -> p c t", t=TSTRIPE),
                                   sem=f"st_i9{st % 2}")
            p.wait("vector", ("ser_tensor", p.val("ser_tensor")))
            twd = dequant9(DWP, DSC, 0)
            if dbg:
                p.wait("gpsimd", twd)
                p.dma("gpsimd", DBG["d_wdt"], wbig[:, 0:NCHUNK * D], sem="dbg")
            p.wait_all("sync", "st_i90")
            p.wait_all("sync", "st_i91")
            psD8 = [bank[0].ap(), bank[1].ap()]
            ev_d = {}
            dn_mm = {}
            for st in range(NSTRIPE):
                bb, sj = st % 2, st // 2
                ibb = ib9 if st % 2 == 0 else xstr
                if st >= 2:
                    p.wait("sync", dn_mm[st - 2])
                tld = p.dma("sync", ibb.ap().rearrange("p (c t) -> p c t", t=TSTRIPE),
                            INT_S.rearrange("(c p) t -> p c t", p=128)
                            [:, :, (bb * NTT + sj) * TSTRIPE:
                             (bb * NTT + sj + 1) * TSTRIPE],
                            sem=f"ld_i9{st % 2}")
                p.wait("tensor", tld)
                if st == 0:
                    p.wait("tensor", twd)
                    if dbg:
                        p.wait("gpsimd", tld)
                        tdbg = p.dma("gpsimd", DBG["d_ib"], ib9[:], sem="dbg")
                        p.wait("sync", tdbg)
                for n in range(NCHUNK):
                    m = st * NCHUNK + n
                    pd_ = psD8[n % 2]
                    if m >= 2:
                        p.wait("tensor", ev_d[m - 2])
                    elif st == 0:
                        p.wait("tensor", ev_u[NSTRIPE * NCHUNK - 2 + n])
                    for c in range(NCHUNK):
                        tmd = p.op("tensor", lambda e, pd_=pd_, n=n, c=c, ibb=ibb:
                            e.matmul(
                                pd_[:],
                                wbig[:, c * 2048 + n * 128:c * 2048 + (n + 1) * 128],
                                ibb[:, c * TSTRIPE:(c + 1) * TSTRIPE],
                                start=(c == 0), stop=(c == NCHUNK - 1)))
                    p.wait("vector", tmd)
                    ev_d[m] = p.op("vector", lambda e, pd_=pd_: e.tensor_copy(
                        ffb9[:], pd_[:]))
                    p.wait("gpsimd", ev_d[m])
                    tsf = p.dma("gpsimd",
                                RSI[sj][(bb * 4 + n // 4) * 512 + (n % 4) * 128:
                                        (bb * 4 + n // 4) * 512 + (n % 4) * 128 + 128,
                                        :],
                                ffb9[:], sem="st_ff9")
                    p.wait("vector", tsf)
                dn_mm[st] = tmd
                if st % 2 == 1:
                    p.wait("gpsimd", tsf)
                    p.wait_all("gpsimd", "st_ff9")
                    if dbg and sj == 0:
                        p.dma("gpsimd", DBG["d_rsi"], RSI[0], sem="dbg")
                    rs_done[sj] = p.op("gpsimd", lambda e, sj=sj: e.collective_compute(
                        "ReduceScatter", ALU.add, ins=[RSI[sj].opt()],
                        outs=[RSO[sj].opt()], replica_groups=G8))
        p.barrier()
        if stages < 11:
            stmr.close()
            p.realize()
            return nc

        mark("S9")
        with contextlib.ExitStack() as stA:
            sA = lambda n, sh, dt: stA.enter_context(nc.sbuf_tensor(n, sh, dt))
            y1A = sA("y1A", [128, 512], F32)
            x1A = sA("x1A", [128, 512], F32)
            ffA = sA("ffA", [128, 512], F32)
            lnwA = sA("lnwA", [128, Q4], F32)
            lnbA = sA("lnbA", [128, Q4], F32)
            outA = sA("outA", [128, 512], F32)
            trA = sA("trA", [128, 128], F32)
            mubA = sA("mubA", [128, T], F32)
            rsbA = sA("rsbA", [128, T], F32)
            p.dma("sync", mubA[:], MU_S[0:1, :].broadcast_to((128, T)), sem="ld_lnA")
            p.dma("sync", rsbA[:], MU_S[1:2, :].broadcast_to((128, T)), sem="ld_lnA")
            for c in range(Q4):
                p.dma("sync", lnwA[:, c:c + 1], LNWS[c * 128:(c + 1) * 128, :],
                      sem="ld_lnA")
                p.dma("sync", lnbA[:, c:c + 1], LNBS[c * 128:(c + 1) * 128, :],
                      sem="ld_lnA")
            p.wait_all("vector", "ld_lnA")
            stA_t = {}
            trev = {0: None, 1: None}
            for j in range(NTT):
                sj = j
                for q in range(Q4):
                    m = j * Q4 + q
                    if m >= 1:
                        p.wait("sync", ("ser_vector", p.val("ser_vector")))
                    tly = p.dma("sync", y1A[:],
                                Y1_S[q * 128:(q + 1) * 128, j * 512:(j + 1) * 512],
                                sem=f"ldyA{m % 2}")
                    tlx = p.dma("sync", x1A[:],
                                X1_S[q * 128:(q + 1) * 128, j * 512:(j + 1) * 512],
                                sem=f"ldxA{m % 2}")
                    tlf = p.dma("sync", ffA[:],
                                RSO[sj][q * 128:(q + 1) * 128, :], sem=f"ldfA{m % 2}")
                    if sj in rs_done:
                        p.wait("sync", rs_done[sj])
                    p.wait("vector", tly)
                    p.wait("vector", tlx)
                    p.wait("vector", tlf)
                    if m >= 2:
                        p.wait("vector", stA_t[m - 2])
                    p.op("vector", lambda e, j=j: e.tensor_sub(
                        outA[:], y1A[:], mubA[:, j * 512:(j + 1) * 512]))
                    p.op("vector", lambda e, j=j: e.tensor_mul(
                        outA[:], outA[:], rsbA[:, j * 512:(j + 1) * 512]))
                    p.op("vector", lambda e, q=q: e.tensor_scalar(
                        outA[:], outA[:], lnwA[:, q:q + 1], lnbA[:, q:q + 1],
                        ALU.mult, ALU.add))
                    p.op("vector", lambda e: e.tensor_add(outA[:], outA[:], x1A[:]))
                    tfin = p.op("vector", lambda e: e.tensor_add(
                        outA[:], outA[:], ffA[:]))
                    for tt_ in range(4):
                        pt = bank[4 + tt_ % 2].ap()[:, 0:128]
                        if trev[tt_ % 2] is not None:
                            p.wait("tensor", trev[tt_ % 2])
                        p.wait("tensor", tfin)
                        ttp = p.op("tensor", lambda e, pt=pt, tt_=tt_: e.transpose(
                            pt, outA[:, tt_ * 128:(tt_ + 1) * 128], idf[:]))
                        p.wait("vector", ttp)
                        tev = p.op("vector", lambda e, pt=pt: e.tensor_copy(
                            trA[:], pt))
                        trev[tt_ % 2] = tev
                        p.wait("gpsimd", tev)
                        tso = p.dma("gpsimd",
                                    OUTR[j * 512 + tt_ * 128:
                                         j * 512 + (tt_ + 1) * 128,
                                         q * 128:(q + 1) * 128],
                                    trA[:], sem="st_out")
                        p.wait("vector", tso)
                    stA_t[m] = tso
        if dbg:
            p.barrier()
            p.dma("gpsimd", DBG["d_x1"], X1_S, sem="dbg")
            p.dma("gpsimd", DBG["d_xn1g"], XN1G, sem="dbg")
            p.dma("gpsimd", DBG["d_r"], RKVG_S["r"], sem="dbg")
            p.dma("gpsimd", DBG["d_g"], RKVG_S["g"], sem="dbg")
            p.dma("gpsimd", DBG["d_y1"], Y1_S, sem="dbg")
            p.dma("gpsimd", DBG["d_rso0"], RSO[0], sem="dbg")
            p.dma("gpsimd", DBG["d_int"], INT_S[:, 0:512], sem="dbg")
            p.dma("gpsimd", DBG["d_mu"], MU_S, sem="dbg")
            p.wait_all("gpsimd", "dbg")
        p.wait_all("gpsimd", "st_out")
        mark(None)

        p.realize()
        return nc


def _fused_inmaps(inputs):
    idb = np.eye(128, dtype=bf16)
    idf = np.eye(128, dtype=np.float32)
    x = inputs["L0_attn_x"]
    xT = [np.ascontiguousarray(x[b].T) for b in range(2)]
    wbcT = np.ascontiguousarray(inputs["L0_attn_W_bc"].T)
    wdtT = np.ascontiguousarray(inputs["L0_attn_W_dt"].T)
    w0T = np.ascontiguousarray(inputs["L0_ffn_weight"].T)
    wT = {s: np.ascontiguousarray(inputs[f"L1_attn_W{s}"].T) for s in "rkvg"}
    woT = np.ascontiguousarray(inputs["L1_attn_Wo"].T)
    td = inputs["L1_attn_time_decay"]
    tf = inputs["L1_attn_time_first"]
    maps = []
    for c in range(8):
        b, q = c // 4, c % 4
        ds = slice(q * 512, (q + 1) * 512)
        m = {
            "xt": xT[b], "xts": np.ascontiguousarray(xT[b][ds]),
            "wbcT": wbcT,
            "wdtTs": np.ascontiguousarray(wdtT[:, ds]),
            "bdts": np.ascontiguousarray(inputs["L0_attn_b_dt"][ds].reshape(512, 1)),
            "alogs": np.ascontiguousarray(inputs["L0_attn_A_log"][ds]),
            "dpars": np.ascontiguousarray(inputs["L0_attn_D_param"][ds].reshape(512, 1)),
            "w0T": w0T,
            "w0Ts": np.ascontiguousarray(w0T[:, ds]),
            "b0s": np.ascontiguousarray(inputs["L0_ffn_bias"][ds].reshape(512, 1)),
            "woTs": np.ascontiguousarray(woT[:, ds]),
            "wdec": np.ascontiguousarray(td[4 * q:4 * q + 4].T),
            "tfir": np.ascontiguousarray(tf[4 * q:4 * q + 4].T),
            "lnws": np.ascontiguousarray(inputs["L1_attn_ln_w"][ds].reshape(512, 1)),
            "lnbs": np.ascontiguousarray(inputs["L1_attn_ln_b"][ds].reshape(512, 1)),
            "gwp": np.ascontiguousarray(inputs["L1_ffn_gate_W_packed"][c * 2048:(c + 1) * 2048]),
            "gsc": np.ascontiguousarray(inputs["L1_ffn_gate_scales"][c * 2048:(c + 1) * 2048]),
            "uwp": np.ascontiguousarray(inputs["L1_ffn_up_W_packed"][c * 2048:(c + 1) * 2048]),
            "usc": np.ascontiguousarray(inputs["L1_ffn_up_scales"][c * 2048:(c + 1) * 2048]),
            "dwp": np.ascontiguousarray(inputs["L1_ffn_down_W_packed"][:, c * 1024:(c + 1) * 1024]),
            "dsc": np.ascontiguousarray(inputs["L1_ffn_down_scales"][:, c * 64:(c + 1) * 64]),
            "identb": idb, "identf": idf,
        }
        for s in "rkvg":
            m[f"w{s}Ts"] = np.ascontiguousarray(wT[s][:, 512 * q:512 * (q + 1)])
            m[f"tm{s}"] = np.ascontiguousarray(inputs[f"L1_attn_tm_{s}"].reshape(D, 1))
        maps.append(m)
    return maps


def kernel(**inputs):
    inputs = {k: np.asarray(v) for k, v in inputs.items()}
    nc = build_fused()
    res = run_bass_kernel_spmd(nc, _fused_inmaps(inputs), core_ids=list(range(8)))
    out = np.zeros((B, T, D), np.float32)
    for c in range(8):
        b, q = c // 4, c % 4
        out[b, :, q * 512:(q + 1) * 512] = res.results[c]["outr"]
    return out
